# version 17
# baseline (speedup 1.0000x reference)
import math
import contextlib
import numpy as np
import ml_dtypes
import concourse.bass as bass
import concourse.mybir as mybir

F32 = mybir.dt.float32
BF16 = mybir.dt.bfloat16
I32 = mybir.dt.int32
I16 = mybir.dt.int16
AF = mybir.ActivationFunctionType
ALU = mybir.AluOpType
AX = mybir.AxisListType

L, DM, H, DH, R, IF = 2048, 1024, 16, 64, 512, 2816
NTC, NKC, NIC = L // 128, DM // 128, IF // 128
GRPS = [(g * 512, 512) for g in range(5)] + [(2560, 256)]
EPS = 1e-5
QSCALE = 1.0 / math.sqrt(R)
ENGINES = ("sync", "tensor", "vector", "scalar", "gpsimd")
QNAMES = ("u", "p1", "c1", "c2", "B", "s1", "s2", "s3")


def host_constants():
    c = {}
    c["ident_f32"] = np.eye(128, dtype=np.float32)
    c["ident_bf16"] = np.eye(128, dtype=ml_dtypes.bfloat16)
    c["one_f32"] = np.ones((1, 1), dtype=np.float32)
    c["onescol_f32"] = np.ones((1, 128), dtype=np.float32)
    crows = np.zeros((128, 2), dtype=np.float32)
    crows[:, 0] = 4.0
    c["crows"] = crows

    def fbasis(t, m):
        s = math.sqrt((1.0 if m == 0 else 2.0) / L)
        return np.cos(np.pi * (np.abs(t) + 0.5) * m / L) * s

    rr = np.arange(128)[:, None]
    cc = np.arange(256)[None, :]
    t = cc - rr - 64
    band = (t >= -31) & (t <= 32)
    Cm = np.stack([fbasis(t, m) * band for m in range(6)])
    c["slab_basis"] = np.ascontiguousarray(Cm.transpose(1, 0, 2).reshape(128, 6 * 256)).astype(np.float32)
    kk = np.arange(64)[:, None]
    qq = np.arange(32)[None, :]
    t0 = kk - qq
    C0 = np.stack([fbasis(t0, m) * (t0 >= 32) for m in range(6)])
    kL = (1984 + np.arange(64))[:, None]
    qL = (2016 + np.arange(32))[None, :]
    tL = qL - kL
    mL = (tL >= 33) & (kL <= 2014)
    CL = np.stack([fbasis(tL, m) * mL for m in range(6)])
    c["corr0_basis"] = np.ascontiguousarray(C0.transpose(1, 0, 2).reshape(64, 6 * 32)).astype(np.float32)
    c["corrL_basis"] = np.ascontiguousarray(CL.transpose(1, 0, 2).reshape(64, 6 * 32)).astype(np.float32)
    return c


class SemW:
    __slots__ = ("h", "_v")
    def __init__(self, h):
        self.h, self._v = h, 0


class Emitter:
    def __init__(self, nc, es):
        self.nc, self.es = nc, es
        self.prog = {e: [] for e in ENGINES}

    def sem(self, name):
        return SemW(self.es.enter_context(self.nc.semaphore(name)))

    def emit(self, eng, fn, waits=(), inc=None, incn=1):
        waits = [(s, (v if v is not None else s._v)) for (s, v) in waits]
        if inc is not None:
            inc._v += incn
        def thunk(e, fn=fn, waits=waits, inc=inc, incn=incn):
            for s, v in waits:
                if v > 0:
                    e.wait_ge(s.h, v)
            ins = fn(e)
            if inc is not None:
                assert ins is not None, "inc requires fn to return its last instruction"
                ins.then_inc(inc.h, incn)
        self.prog[eng].append(thunk)
        return inc._v if inc is not None else None

    def wait_only(self, eng, waits):
        waits = [(s, (v if v is not None else s._v)) for (s, v) in waits]
        def thunk(e, waits=waits):
            for s, v in waits:
                if v > 0:
                    e.wait_ge(s.h, v)
        self.prog[eng].append(thunk)

    def fenced(self, eng, fn, sem, waits=()):
        v = self.emit(eng, fn, waits=waits, inc=sem)
        self.wait_only(eng, [(sem, v)])
        return v

    def run(self, block):
        block.sync(lambda e: [f(e) for f in self.prog["sync"]])
        block.tensor(lambda e: [f(e) for f in self.prog["tensor"]])
        block.vector(lambda e: [f(e) for f in self.prog["vector"]])
        block.scalar(lambda e: [f(e) for f in self.prog["scalar"]])
        block.gpsimd(lambda e: [f(e) for f in self.prog["gpsimd"]])


def build_core_program(NB=2, debug_dump=()):
    nc = bass.Bass("TRN2", target_bir_lowering=False)
    es = contextlib.ExitStack()
    nc._keepalive_es = es

    def din(name, shape, dtype=F32):
        return nc.dram_tensor(name, shape, dtype, kind="ExternalInput").ap()

    x_d = din("x", [NB, L, DM])
    wq_d, bq_d = din("wq", [DM, DM]), din("bq", [DM])
    wk_d, bk_d = din("wk", [DM, DM]), din("bk", [DM])
    wv_d, bv_d = din("wv", [DM, DM]), din("bv", [DM])
    coef_d = din("coef", [H, 6])
    qproj_d, kproj_d = din("qproj", [H, DH, R]), din("kproj", [H, DH, R])
    wgsh_d = din("wgsh", [IF // 8, DM])
    wush_d = din("wush", [IF // 8, DM])
    wdsh_d = din("wdsh", [DM, IF // 8])
    identf_d = din("ident_f32", [128, 128])
    identb_d = din("ident_bf16", [128, 128], BF16)
    onef_d = din("one_f32", [1, 1])
    onescolf_d = din("onescol_f32", [1, 128])
    crows_d = din("crows", [128, 2])
    slabB_d = din("slab_basis", [128, 6 * 256])
    corr0B_d = din("corr0_basis", [64, 6 * 32])
    corrLB_d = din("corrL_basis", [64, 6 * 32])

    out_d = nc.dram_tensor("out", [NB, L, DM], F32, kind="ExternalOutput").ap()

    ctx_dram = nc.dram_tensor("ctx_scr", [L, DM], BF16, kind="Internal").ap()
    SH = IF // 8
    wQl = {}
    wQ_gout = {}
    wQ_ld = {}
    for tag in ("g", "u"):
        wQl[tag] = nc.dram_tensor("wQl_%s" % tag, [DM, SH], BF16, kind="Internal").ap()
        flat = nc.dram_tensor("wQg_%s" % tag, [8 * DM * SH], BF16, kind="Internal").ap()
        wQ_gout[tag] = flat.rearrange("(a s) -> a s", s=SH)
        wQ_ld[tag] = flat.rearrange("(r mc p s) -> mc p r s", r=8, mc=NKC, p=128)
    wQl["d"] = nc.dram_tensor("wQl_d", [SH, DM], BF16, kind="Internal").ap()
    flat_d = nc.dram_tensor("wQg_d", [IF * DM], BF16, kind="Internal").ap()
    wQ_gout["d"] = flat_d.rearrange("(a m) -> a m", m=DM)
    wQ_ld["d"] = flat_d.rearrange("(a m) -> a m", m=DM)

    dbg = {}
    def ddump(name, shape, dtype):
        if name in debug_dump:
            dbg[name] = nc.dram_tensor("dbg_" + name, list(shape), dtype, kind="ExternalOutput").ap()
    ddump("xln", (128, NKC * L), BF16)
    ddump("qT", (128, NKC * L), BF16)
    ddump("kT", (128, NKC * L), BF16)
    ddump("vaug", (128, NTC * 1040), BF16)
    ddump("kp0", (128, 4 * L), BF16)
    ddump("pt00", (128, 16 * 512), BF16)
    ddump("slab0", (128, 256), BF16)
    ddump("ctx", (128, NTC * DM), BF16)
    ddump("wgq", (128, NIC * DM), BF16)
    ddump("x1t", (128, DM), F32)
    ddump("hq", (128, IF), BF16)

    E = Emitter(nc, es)
    sb = lambda name, shape, dtype: es.enter_context(nc.sbuf_tensor(name, shape, dtype))

    identb = sb("identb", [128, 128], BF16)
    identf = sb("identf", [128, 128], F32)
    onef = sb("onef", [1, 1], F32)
    onescolb = sb("onescolb", [1, 128], BF16)
    crows = sb("crows_sb", [128, 2], F32)
    slabs = sb("slabs", [128, H * 256], BF16)
    corr0s = sb("corr0s", [64, H * 32], BF16)
    corrLs = sb("corrLs", [64, H * 32], BF16)
    bqT = sb("bqT", [128, NKC], F32)
    bkT = sb("bkT", [128, NKC], F32)
    lnst = sb("lnst", [128, 12], F32)
    sqjunk = sb("sqjunk", [128, DM], BF16)
    mrow = sb("mrow", [128, NTC], F32)
    rrow = sb("rrow", [128, NTC], F32)
    nbias = sb("nbias", [128, NTC], F32)

    s_ld = E.sem("s_ld")
    s_cc = E.sem("s_cc")
    s_ldx = E.sem("s_ldx")
    s_mm = E.sem("s_mm")
    s_ve = E.sem("s_ve")
    s_se = E.sem("s_se")
    s_ge = E.sem("s_ge")
    s_st = E.sem("s_st")

    QTMP = {}

    def quant_head(src_ap, Wd, T, mi, waits_v=(), P=128):
        nb = Wd // 16
        v3 = lambda ap: ap.rearrange("p (b k) -> p b k", k=16)
        mT = T["m"]
        bmv = mT[0:P, 192 * mi:192 * mi + nb]
        rsv = mT[0:P, 192 * mi + 64:192 * mi + 64 + nb]
        shv = mT[0:P, 192 * mi + 128:192 * mi + 128 + nb]
        v1 = E.emit("vector", lambda e: e.tensor_reduce(
            bmv, v3(src_ap), AX.X, ALU.max, apply_absolute_value=True),
            waits=list(waits_v) + [(s_ge, T.get("mge", 0)), (s_se, T.get("mse", 0))], inc=s_ve)
        yield
        E.wait_only("vector", [(s_ve, v1)])
        v2 = E.emit("vector", lambda e: e.tensor_scalar(rsv, bmv, 1.0 / 6.0, 1e-12, ALU.mult, ALU.max), inc=s_ve)
        yield
        E.wait_only("vector", [(s_ve, v2)])
        E.emit("vector", lambda e: e.tensor_scalar(shv, rsv, 0.5, None, ALU.mult))
        v3b = E.emit("vector", lambda e: e.reciprocal(bmv, rsv), inc=s_ve)
        yield
        E.wait_only("vector", [(s_ve, v3b)])

    def quant_tail(src_ap, out_ap, Wd, T, mi, waits_q=(), P=128):
        nb = Wd // 16
        v3 = lambda ap: ap.rearrange("p (b k) -> p b k", k=16)
        bc = lambda ap: ap.unsqueeze(2).broadcast_to((P, nb, 16))
        fT, iT, mT = T["f"], T["i"], T["m"]
        sl = lambda j: (T["c2"][0:P, 0:Wd] if j == 3 else fT[0:P, 512 * j:512 * j + Wd])
        il = lambda j: iT[0:P, 512 * j:512 * j + Wd]
        bmv = mT[0:P, 192 * mi:192 * mi + nb]
        shv = mT[0:P, 192 * mi + 128:192 * mi + 128 + nb]
        uv = E.emit("vector", lambda e: e.tensor_tensor(v3(sl(0)), v3(src_ap), bc(bmv), ALU.mult), inc=s_ve)
        sv = E.emit("scalar", lambda e: e.activation(sl(1), sl(0), AF.Relu,
                                                     bias=crows[0:P, 0:1], scale=2.0),
                    waits=[(s_ve, uv), (s_ge, T.get("mge", 0))], inc=s_se)
        T["mse"] = sv
        E.emit("vector", lambda e: e.tensor_scalar(il(1), sl(0), -4.0, 4.0, ALU.max, ALU.min))
        E.emit("vector", lambda e: e.tensor_scalar(il(2), sl(0), -2.0, 2.0, ALU.max, ALU.min))
        E.emit("vector", lambda e: e.tensor_scalar(sl(3), sl(0), -5.0, -4.0, ALU.is_lt, ALU.mult))
        cv = E.emit("vector", lambda e: e.tensor_scalar(sl(2), sl(0), 5.0, 4.0, ALU.is_gt, ALU.mult), inc=s_ve)
        vA = E.emit("vector", lambda e: e.tensor_scalar(il(0), sl(1), 8.0, None, ALU.min),
                    waits=[(s_se, sv)], inc=s_ve)
        gv = E.emit("gpsimd", lambda e: e.tensor_tensor(sl(1), il(1), il(2), ALU.subtract),
                    waits=[(s_ve, vA)], inc=s_ge)
        gv2 = E.emit("gpsimd", lambda e: e.tensor_tensor(sl(3), sl(2), sl(3), ALU.add),
                     waits=[(s_ve, cv)], inc=s_ge)
        T["mge"] = gv2
        E.emit("vector", lambda e: e.scalar_tensor_tensor(sl(0), sl(1), 2.0, il(0), ALU.mult, ALU.add),
               waits=[(s_ge, gv)])
        E.emit("vector", lambda e: e.scalar_tensor_tensor(sl(2), sl(0), -4.0, sl(3), ALU.add, ALU.add),
               waits=[(s_ge, gv2)])
        return E.emit("vector", lambda e: e.tensor_tensor(v3(out_ap), v3(sl(2)), bc(shv), ALU.mult),
                      waits=list(waits_q), inc=s_ve)

    def quant_pair(a, bq):
        P_ = lambda x: x[7] if len(x) > 7 else 128
        gens = [quant_head(x[0], x[2], x[3], x[4], waits_v=x[5], P=P_(x)) for x in (a, bq) if x]
        alive = list(gens)
        while alive:
            for g in list(alive):
                try:
                    next(g)
                except StopIteration:
                    alive.remove(g)
        last = None
        for x in (a, bq):
            if x:
                last = quant_tail(x[0], x[1], x[2], x[3], x[4], waits_q=x[6], P=P_(x))
        return last

    def emit_quant(src_ap, out_ap, Wd, waits_v=(), waits_q=(), T=None):
        if T is None:
            T = QTMP
        return quant_pair((src_ap, out_ap, Wd, T, 0, list(waits_v), list(waits_q)), None)

    def emit_ln(x_ap, m_dst, r_dst, nb_dst, waits0=()):
        E.fenced("vector", lambda e: e.tensor_reduce(lnst[:, 0:1], x_ap, AX.X, ALU.add),
                 s_ve, waits=list(waits0))
        mv = E.emit("vector", lambda e: e.tensor_scalar(m_dst, lnst[:, 0:1], 1.0 / DM, None, ALU.mult), inc=s_ve)
        sqv = E.emit("scalar", lambda e: e.activation(sqjunk[:], x_ap, AF.Square, bias=m_dst,
                                                      scale=-1.0, accum_out=lnst[:, 1:2]),
                     waits=[(s_ve, mv)], inc=s_se)
        vv = E.emit("vector", lambda e: e.tensor_scalar(lnst[:, 2:3], lnst[:, 1:2], 1.0 / DM, EPS, ALU.mult, ALU.add),
                    waits=[(s_se, sqv)], inc=s_ve)
        sq2 = E.emit("scalar", lambda e: e.activation(lnst[:, 3:4], lnst[:, 2:3], AF.Sqrt),
                     waits=[(s_ve, vv)], inc=s_se)
        E.fenced("vector", lambda e: e.reciprocal(lnst[:, 4:5], lnst[:, 3:4]), s_ve, waits=[(s_se, sq2)])
        E.fenced("vector", lambda e: e.tensor_tensor(lnst[:, 5:6], lnst[:, 4:5], lnst[:, 4:5], ALU.mult), s_ve)
        E.fenced("vector", lambda e: e.tensor_tensor(lnst[:, 6:7], lnst[:, 5:6], lnst[:, 2:3], ALU.mult), s_ve)
        E.fenced("vector", lambda e: e.tensor_scalar(lnst[:, 6:7], lnst[:, 6:7], -0.5, 1.5, ALU.mult, ALU.add), s_ve)
        E.fenced("vector", lambda e: e.tensor_tensor(r_dst, lnst[:, 4:5], lnst[:, 6:7], ALU.mult), s_ve)
        E.fenced("vector", lambda e: e.tensor_tensor(lnst[:, 7:8], m_dst, r_dst, ALU.mult), s_ve)
        return E.emit("vector", lambda e: e.tensor_scalar(nb_dst, lnst[:, 7:8], -1.0, None, ALU.mult), inc=s_ve)

    with contextlib.ExitStack() as _es0:
        slabB = _es0.enter_context(nc.sbuf_tensor("slabB", [128, 6 * 256], F32))
        corr0B = _es0.enter_context(nc.sbuf_tensor("corr0B", [64, 6 * 32], F32))
        corrLB = _es0.enter_context(nc.sbuf_tensor("corrLB", [64, 6 * 32], F32))
        brow = _es0.enter_context(nc.sbuf_tensor("brow", [1, 3 * DM], F32))
        coefrow = _es0.enter_context(nc.sbuf_tensor("coefrow", [1, 6 * H], F32))
        coefb = _es0.enter_context(nc.sbuf_tensor("coefb", [128, 6], F32))
        qscale_t = _es0.enter_context(nc.sbuf_tensor("qscale_t", [1, 1], F32))

        onescolf = _es0.enter_context(nc.sbuf_tensor("onescolf", [1, 128], F32))
        slab_acc = _es0.enter_context(nc.sbuf_tensor("slab_acc", [128, 2 * 256], F32))
        wchunk = _es0.enter_context(nc.sbuf_tensor("wchunk", [128, 2 * 3072], F32))
        wqn = _es0.enter_context(nc.sbuf_tensor("wqn", [128, 2 * 3072], BF16))
        wqnT = _es0.enter_context(nc.sbuf_tensor("wqnT", [128, 2 * 3072], BF16))
        ps_b0 = _es0.enter_context(nc.psum_tensor("ps_b0", [128, 16], F32))
        ps_cf = _es0.enter_context(nc.psum_tensor("ps_cf", [128, 6], F32))
        ps_w = (_es0.enter_context(nc.psum_tensor("ps_w0", [128, 128], BF16)),
                _es0.enter_context(nc.psum_tensor("ps_w1", [128, 128], BF16)))
        for dst, src in ((identf, identf_d), (identb, identb_d), (onef, onef_d),
                         (onescolf, onescolf_d), (crows, crows_d), (slabB, slabB_d),
                         (corr0B, corr0B_d), (corrLB, corrLB_d)):
            E.emit("sync", lambda e, d=dst, s=src: e.dma_start(out=d[:], in_=s[:]), inc=s_ld, incn=16)
        E.emit("sync", lambda e: e.dma_start(out=brow[0:1, 0:DM], in_=bq_d.unsqueeze(0)), inc=s_ld, incn=16)
        E.emit("sync", lambda e: e.dma_start(out=brow[0:1, DM:2 * DM], in_=bk_d.unsqueeze(0)), inc=s_ld, incn=16)
        E.emit("sync", lambda e: e.dma_start(out=brow[0:1, 2 * DM:3 * DM], in_=bv_d.unsqueeze(0)), inc=s_ld, incn=16)
        for h in range(H):
            E.emit("sync", lambda e, h=h: e.dma_start(out=coefrow[0:1, 6 * h:6 * h + 6], in_=coef_d[h:h + 1, :]), inc=s_ld, incn=16)
        HCR = (H // 2) * R
        ld_consts = s_ld._v
        E.emit("vector", lambda e: e.tensor_copy(onescolb[:], onescolf[:]), waits=[(s_ld, ld_consts)], inc=s_ve)
        E.emit("vector", lambda e: e.memset(qscale_t[:], QSCALE), inc=s_ve)
        ve_consts = s_ve._v

        for i in range(NKC):
            E.emit("tensor", lambda e, i=i: e.matmul(
                ps_b0[:, i:i + 1], brow[0:1, 128 * i:128 * (i + 1)], qscale_t[:],
                start=True, stop=True), waits=[(s_ld, ld_consts), (s_ve, ve_consts)])
        mm_b = 0
        for i in range(NKC):
            mm_b = E.emit("tensor", lambda e, i=i: e.matmul(
                ps_b0[:, 8 + i:9 + i], brow[0:1, DM + 128 * i:DM + 128 * (i + 1)], onef[:],
                start=True, stop=True), inc=(s_mm if i == NKC - 1 else None))
        E.emit("vector", lambda e: e.tensor_copy(bqT[:], ps_b0[:, 0:8]), waits=[(s_mm, mm_b)])
        ve_bias = E.emit("vector", lambda e: e.tensor_copy(bkT[:], ps_b0[:, 8:16]), inc=s_ve)

        for h in range(H):
            mmv = E.emit("tensor", lambda e, h=h: e.matmul(
                ps_cf[:], onescolf[:], coefrow[0:1, 6 * h:6 * h + 6],
                start=True, stop=True), waits=[(s_ve, None)], inc=s_mm)
            E.fenced("vector", lambda e: e.tensor_copy(coefb[:], ps_cf[:]), s_ve, waits=[(s_mm, mmv)])
            E.emit("vector", lambda e: e.tensor_scalar(
                slab_acc[:, 0:256], slabB[:, 0:256], coefb[:, 0:1], None, ALU.mult))
            for m in range(1, 6):
                last = m == 5
                E.emit("vector", lambda e, m=m, h=h, last=last: e.scalar_tensor_tensor(
                    (slabs[:, 256 * h:256 * (h + 1)] if last else slab_acc[:, 256 * (m % 2):256 * (m % 2) + 256]),
                    slabB[:, 256 * m:256 * (m + 1)], coefb[:, m:m + 1],
                    slab_acc[:, 256 * ((m - 1) % 2):256 * ((m - 1) % 2) + 256],
                    ALU.mult, ALU.add), inc=(s_ve if last else None))
            for Bt, St in ((corr0B, corr0s), (corrLB, corrLs)):
                E.fenced("vector", lambda e, Bt=Bt: e.tensor_scalar(
                    slab_acc[0:64, 0:32], Bt[:, 0:32], coefb[0:64, 0:1], None, ALU.mult), s_ve)
                for m in range(1, 6):
                    last = m == 5
                    E.fenced("vector", lambda e, m=m, h=h, last=last, Bt=Bt, St=St: e.scalar_tensor_tensor(
                        (St[:, 32 * h:32 * (h + 1)] if last else slab_acc[0:64, 32 * (m % 2):32 * (m % 2) + 32]),
                        Bt[:, 32 * m:32 * (m + 1)], coefb[0:64, m:m + 1],
                        slab_acc[0:64, 32 * ((m - 1) % 2):32 * ((m - 1) % 2) + 32],
                        ALU.mult, ALU.add), s_ve)
        ve_slabs = s_ve._v
        if "slab0" in dbg:
            E.emit("sync", lambda e: e.dma_start(out=dbg["slab0"], in_=slabs[:, 0:256]),
                   waits=[(s_ve, ve_slabs)], inc=s_st, incn=16)

        QT2 = []
        for si in range(2):
            QT2.append({
                "f": _es0.enter_context(nc.sbuf_tensor("q0f%d" % si, [128, 3 * 512], F32)),
                "i": _es0.enter_context(nc.sbuf_tensor("q0i%d" % si, [128, 3 * 512], I16)),
                "c2": _es0.enter_context(nc.sbuf_tensor("q0c2%d" % si, [128, 512], F32)),
                "m": _es0.enter_context(nc.sbuf_tensor("q0m%d" % si, [128, 384], F32)),
            })
        STILES = ((0, 0, 128), (1, 128, 128), (2, 256, 96))
        st_wql = {0: 0, 1: 0}
        ve_wq = {0: ve_consts, 1: ve_consts}
        mm_wqT = {0: 0, 1: 0}
        se_evac = [0, 0]
        for ti, (tag, src_d) in enumerate((("g", wgsh_d), ("u", wush_d), ("d", wdsh_d))):
            par = ti % 2
            base = 3072 * par
            if tag != "d":
                lw = []
                for si, s0, sw in STILES:
                    lw.append(E.emit("sync", lambda e, s=src_d, si=si, s0=s0, sw=sw, base=base: e.dma_start(
                        out=wchunk[0:sw, base + 1024 * si: base + 1024 * si + 1024],
                        in_=s[s0:s0 + sw, :]),
                        waits=[(s_ve, ve_wq[par])], inc=s_ld, incn=16))
                for si, s0, sw in STILES:
                    pr = []
                    for k2, g0 in enumerate((0, 512)):
                        pr.append((wchunk[0:sw, base + 1024 * si + g0: base + 1024 * si + g0 + 512],
                                   wqn[0:sw, base + 1024 * si + g0: base + 1024 * si + g0 + 512], 512,
                                   QT2[k2], 0, [(s_ld, lw[si])], [(s_mm, mm_wqT[par])], sw))
                    ve_wq[par] = quant_pair(pr[0], pr[1])
                ve_q = ve_wq[par]
                for si, s0, sw in STILES:
                    for mc in range(NKC):
                        pp = (si * NKC + mc) % 2
                        mt = E.emit("tensor", lambda e, si=si, sw=sw, mc=mc, pp=pp, base=base: e.transpose(
                            ps_w[pp][:, 0:sw],
                            wqn[0:sw, base + 1024 * si + 128 * mc: base + 1024 * si + 128 * (mc + 1)],
                            identb[0:sw, 0:sw]),
                            waits=[(s_ve, ve_q), (s_se, se_evac[pp]), (s_ld, ld_consts)], inc=s_mm)
                        se_evac[pp] = E.emit("scalar", lambda e, s0=s0, sw=sw, mc=mc, pp=pp, base=base: e.activation(
                            wqnT[:, base + SH * mc + s0: base + SH * mc + s0 + sw],
                            ps_w[pp][:, 0:sw], AF.Copy),
                            waits=[(s_mm, mt), (s_st, st_wql[par])], inc=s_se)
                mm_wqT[par] = s_mm._v
                se_done = s_se._v
                st_wql[par] = E.emit("sync", lambda e, tag=tag, base=base: e.dma_start(
                    out=wQl[tag].rearrange("(mc p) s -> p mc s", p=128),
                    in_=wqnT[:, base: base + 2816].rearrange("p (mc s) -> p mc s", mc=NKC)),
                    waits=[(s_se, se_done)], inc=s_st, incn=16)
            else:
                lwd0 = E.emit("sync", lambda e, s=src_d, base=base: e.dma_start(
                    out=wchunk[:, base: base + 2816].rearrange("p (mc s) -> p mc s", mc=NKC),
                    in_=s.rearrange("(mc p) s -> p mc s", p=128)),
                    waits=[(s_ve, ve_wq[par])], inc=s_ld, incn=16)
                for gi in range(0, NKC, 2):
                    pr = []
                    for k2 in range(2):
                        g0 = SH * (gi + k2)
                        pr.append((wchunk[:, base + g0: base + g0 + SH],
                                   wqn[:, base + g0: base + g0 + SH], SH,
                                   QT2[k2], 0, [(s_ld, lwd0)], [(s_mm, mm_wqT[par])], 128))
                    ve_wq[par] = quant_pair(pr[0], pr[1])
                ve_q = ve_wq[par]
                for mc in range(NKC):
                    for si, s0, sw in STILES:
                        pp = (mc * 3 + si) % 2
                        mt = E.emit("tensor", lambda e, mc=mc, s0=s0, sw=sw, pp=pp, base=base: e.transpose(
                            ps_w[pp][0:sw, :],
                            wqn[:, base + SH * mc + s0: base + SH * mc + s0 + sw],
                            identb[:, :]),
                            waits=[(s_ve, ve_q), (s_se, se_evac[pp]), (s_ld, ld_consts)], inc=s_mm)
                        se_evac[pp] = E.emit("scalar", lambda e, mc=mc, si=si, sw=sw, pp=pp, base=base: e.activation(
                            wqnT[0:sw, base + 1024 * si + 128 * mc: base + 1024 * si + 128 * (mc + 1)],
                            ps_w[pp][0:sw, :], AF.Copy),
                            waits=[(s_mm, mt), (s_st, st_wql[par])], inc=s_se)
                mm_wqT[par] = s_mm._v
                se_done = s_se._v
                for si, s0, sw in STILES:
                    st_wql[par] = E.emit("sync", lambda e, si=si, s0=s0, sw=sw, base=base: e.dma_start(
                        out=wQl["d"][s0:s0 + sw, :],
                        in_=wqnT[0:sw, base + 1024 * si: base + 1024 * si + 1024]),
                        waits=[(s_se, se_done)], inc=s_st, incn=16)
            E.emit("gpsimd", lambda e, tag=tag: e.collective_compute(
                "AllGather", mybir.AluOpType.bypass,
                replica_groups=[list(range(8))],
                ins=[wQl[tag].opt()], outs=[wQ_gout[tag].opt()]),
                waits=[(s_st, st_wql[par])], inc=s_cc, incn=1)
        cc_wq_done = s_cc._v

    ve_xt_free_start = 0
    st_ctx_rd = 0
    st_ctx_prev = 0
    st_out_last = 0

    for b in range(NB):
        st_bgate = s_st._v
        with contextlib.ExitStack() as _esb:
            qT = _esb.enter_context(nc.sbuf_tensor("qT%d" % b, [128, NKC * L], BF16))
            kT = _esb.enter_context(nc.sbuf_tensor("kT%d" % b, [128, NKC * L], BF16))
            vaug = _esb.enter_context(nc.sbuf_tensor("vaug%d" % b, [128, NTC * 1040], BF16))
            projW = _esb.enter_context(nc.sbuf_tensor("projW%d" % b, [128, 2 * (H // 2) * R], BF16))
            _pw = [(s_mm, s_mm._v), (s_se, s_se._v), (s_ve, s_ve._v), (s_st, st_bgate)]
            for par_ in (0, 1):
                E.emit("gpsimd", lambda e, par_=par_: e.dma_start(
                    out=projW[64 * par_:64 * par_ + 64, 0:HCR].rearrange("d (c r) -> d c r", c=H // 2),
                    in_=qproj_d.rearrange("(c par) d r -> par d c r", par=2)[par_]),
                    waits=_pw, inc=s_ld, incn=16)
                E.emit("gpsimd", lambda e, par_=par_: e.dma_start(
                    out=projW[64 * par_:64 * par_ + 64, HCR:2 * HCR].rearrange("d (c r) -> d c r", c=H // 2),
                    in_=kproj_d.rearrange("(c par) d r -> par d c r", par=2)[par_]),
                    waits=_pw, inc=s_ld, incn=16)
            ld_proj = s_ld._v
            xlnT_ready = [0] * NTC
            with contextlib.ExitStack() as _esa:
                xlnT = _esa.enter_context(nc.sbuf_tensor("xlnT%d" % b, [128, NKC * L], BF16))
                wqkv = _esa.enter_context(nc.sbuf_tensor("wqkv%d" % b, [128, NKC * DM], BF16))
                xt2 = _esa.enter_context(nc.sbuf_tensor("xt2%d" % b, [128, 2 * DM], F32))
                xlntile = _esa.enter_context(nc.sbuf_tensor("xlntile%d" % b, [128, DM], BF16))

                bvrow = _esa.enter_context(nc.sbuf_tensor("bvrow%d" % b, [1, DM], BF16))
                ps_a0 = _esa.enter_context(nc.psum_tensor("ps_a0%d" % b, [128, 512], F32))
                ps_a1 = _esa.enter_context(nc.psum_tensor("ps_a1%d" % b, [128, 512], F32))
                ps_t0 = _esa.enter_context(nc.psum_tensor("ps_t0%d" % b, [128, 128], BF16))
                ps_t1 = _esa.enter_context(nc.psum_tensor("ps_t1%d" % b, [128, 128], BF16))
                ps_a = (ps_a0, ps_a1)
                ps_t = (ps_t0, ps_t1)
                ev_tp = [0, 0]
                xt_free = [ve_xt_free_start] * NTC
                se_xln_prev = 0
                for tc in range(NTC):
                    par = tc % 2
                    ldx = E.emit("sync", lambda e, b=b, tc=tc, par=par: e.dma_start(
                        out=xt2[:, DM * par:DM * (par + 1)], in_=x_d[b, 128 * tc:128 * (tc + 1), :]),
                        waits=[(s_ve, xt_free[tc - 2] if tc >= 2 else ve_xt_free_start),
                               (s_se, se_xln_prev), (s_st, st_bgate)], inc=s_ldx, incn=16)
                    xt = xt2[:, DM * par:DM * (par + 1)]
                    nbv = emit_ln(xt, lnst[:, 8:9], lnst[:, 9:10], lnst[:, 10:11],
                                  waits0=[(s_ldx, ldx)])
                    xt_free[tc] = nbv
                    lv = E.emit("scalar", lambda e, xt=xt: e.activation(
                        xlntile[:], xt, AF.Identity, bias=lnst[:, 10:11], scale=lnst[:, 9:10]),
                        waits=[(s_ve, nbv)], inc=s_se)
                    se_xln_prev = lv
                    for c in range(NKC):
                        tpar = c % 2
                        mt = E.emit("tensor", lambda e, c=c, tpar=tpar: e.transpose(
                            ps_t[tpar][:], xlntile[:, 128 * c:128 * (c + 1)], identb[:]),
                            waits=[(s_se, lv), (s_ve, ev_tp[tpar])], inc=s_mm)
                        ev_tp[tpar] = E.emit("vector", lambda e, c=c, tc=tc, tpar=tpar: e.tensor_copy(
                            xlnT[:, L * c + 128 * tc: L * c + 128 * (tc + 1)], ps_t[tpar][:]),
                            waits=[(s_mm, mt)], inc=s_ve)
                    xlnT_ready[tc] = ev_tp[(NKC - 1) % 2]

                if "xln" in dbg:
                    E.emit("sync", lambda e: e.dma_start(out=dbg["xln"], in_=xlnT[:]),
                           waits=[(s_ve, xlnT_ready[NTC - 1])], inc=s_st, incn=16)

                ev_ps_a = [0, 0]
                for (wsrc, bT, scl, dstT, dbgk) in ((wq_d, bqT, QSCALE, qT, "qT"), (wk_d, bkT, 1.0, kT, "kT")):
                    ldw = [0] * NKC
                    for kc in range(NKC):
                        ldw[kc] = E.emit("gpsimd", lambda e, wsrc=wsrc, kc=kc: e.dma_start(
                            out=wqkv[:, DM * kc:DM * (kc + 1)], in_=wsrc[128 * kc:128 * (kc + 1), :]),
                            waits=([(s_mm, None), (s_st, st_bgate)] if kc == 0 else []) + [(s_se, ev_ps_a[0]), (s_se, ev_ps_a[1])],
                            inc=s_ld, incn=16)
                    for nci in range(NKC):
                        for tcol in range(4):
                            par = (nci * 4 + tcol) % 2
                            def mmq(e, nci=nci, tcol=tcol, par=par):
                                ins = None
                                for kc in range(NKC):
                                    ins = e.matmul(ps_a[par][:],
                                                   wqkv[:, DM * kc + 128 * nci:DM * kc + 128 * (nci + 1)],
                                                   xlnT[:, L * kc + 512 * tcol:L * kc + 512 * (tcol + 1)],
                                                   start=(kc == 0), stop=(kc == NKC - 1))
                                return ins
                            mmv = E.emit("tensor", mmq,
                                         waits=[(s_ld, ldw[NKC - 1]), (s_ve, xlnT_ready[4 * tcol + 3]),
                                                (s_se, ev_ps_a[par])], inc=s_mm)
                            ev_ps_a[par] = E.emit("scalar", lambda e, nci=nci, tcol=tcol, par=par, bT=bT, scl=scl, dstT=dstT: e.activation(
                                dstT[:, L * nci + 512 * tcol:L * nci + 512 * (tcol + 1)], ps_a[par][:],
                                AF.Identity, bias=bT[:, nci:nci + 1], scale=scl),
                                waits=[(s_mm, mmv)], inc=s_se)
                    if dbgk in dbg:
                        E.emit("sync", lambda e, dstT=dstT, dbgk=dbgk: e.dma_start(out=dbg[dbgk], in_=dstT[:]),
                               waits=[(s_se, None)], inc=s_st, incn=16)

                ldw = [0] * NKC
                for kc in range(NKC):
                    ldw[kc] = E.emit("gpsimd", lambda e, kc=kc: e.dma_start(
                        out=wqkv[:, DM * kc:DM * (kc + 1)], in_=wv_d[128 * kc:128 * (kc + 1), :]),
                        waits=([(s_mm, None), (s_st, st_bgate)] if kc == 0 else []) + [(s_se, ev_ps_a[0]), (s_se, ev_ps_a[1])],
                        inc=s_ld, incn=16)
                ld_bv = E.emit("gpsimd", lambda e: e.dma_start(
                    out=bvrow[:], in_=bv_d.unsqueeze(0)), waits=[(s_st, st_bgate)], inc=s_ld, incn=16)
                E.emit("vector", lambda e: e.memset(
                    vaug[:].rearrange("p (t h d) -> p t h d", t=NTC, d=65)[:, :, :, 64:65], 1.0),
                    waits=[(s_mm, None)], inc=s_ve)
                ev_ps_v = [0, 0]
                for tc in range(NTC):
                    for nh in range(2):
                        par = (tc * 2 + nh) % 2
                        def mmv_(e, tc=tc, nh=nh, par=par):
                            for kc in range(NKC):
                                e.matmul(ps_a[par][:],
                                         xlnT[:, L * kc + 128 * tc:L * kc + 128 * (tc + 1)],
                                         wqkv[:, DM * kc + 512 * nh:DM * kc + 512 * (nh + 1)],
                                         start=(kc == 0), stop=False)
                            return e.matmul(ps_a[par][:], onescolb[:], bvrow[0:1, 512 * nh:512 * (nh + 1)],
                                            start=False, stop=True)
                        mmv2 = E.emit("tensor", mmv_,
                                      waits=[(s_ld, ldw[NKC - 1]), (s_ve, ev_ps_v[par]),
                                             (s_se, None), (s_ve, None)], inc=s_mm)
                        dst = vaug[:, 1040 * tc + 520 * nh: 1040 * tc + 520 * (nh + 1)]
                        ev_ps_v[par] = E.emit("vector", lambda e, dst=dst, par=par: e.tensor_copy(
                            dst.rearrange("p (h d) -> p h d", d=65)[:, :, 0:64],
                            ps_a[par][:].rearrange("p (h d) -> p h d", d=64)),
                            waits=[(s_mm, mmv2)], inc=s_ve)
                ve_vaug = s_ve._v
                if "vaug" in dbg:
                    E.emit("sync", lambda e: e.dma_start(out=dbg["vaug"], in_=vaug[:]),
                           waits=[(s_ve, ve_vaug)], inc=s_st, incn=16)

            with contextlib.ExitStack() as _esk:
                kpT = _esk.enter_context(nc.sbuf_tensor("kpT%d" % b, [128, 4 * L], BF16))
                qpT = _esk.enter_context(nc.sbuf_tensor("qpT%d" % b, [128, 4 * 512], BF16))
                ptS = _esk.enter_context(nc.sbuf_tensor("ptS%d" % b, [128, 16 * 512], BF16))

                ctxS = _esk.enter_context(nc.sbuf_tensor("ctxS%d" % b, [128, NTC * DM], BF16))

                ctxf = _esk.enter_context(nc.sbuf_tensor("ctxf%d" % b, [65, 512], F32))

                rs_t = _esk.enter_context(nc.sbuf_tensor("rs_t%d" % b, [128, 2], F32))
                ps_kp0 = _esk.enter_context(nc.psum_tensor("ps_kp0%d" % b, [128, 512], F32))
                ps_kp1 = _esk.enter_context(nc.psum_tensor("ps_kp1%d" % b, [128, 512], F32))
                ps_s0 = _esk.enter_context(nc.psum_tensor("ps_s0%d" % b, [128, 512], F32))
                ps_s1 = _esk.enter_context(nc.psum_tensor("ps_s1%d" % b, [128, 512], F32))
                ps_cx0 = _esk.enter_context(nc.psum_tensor("ps_cx0%d" % b, [65, 512], F32))
                ps_cx1 = _esk.enter_context(nc.psum_tensor("ps_cx1%d" % b, [65, 512], F32))
                ps_ct0 = _esk.enter_context(nc.psum_tensor("ps_ct0%d" % b, [128, 65], F32))
                ps_ct1 = _esk.enter_context(nc.psum_tensor("ps_ct1%d" % b, [128, 65], F32))
                ps_kp = (ps_kp0, ps_kp1)
                ps_s = (ps_s0, ps_s1)
                ps_cx = (ps_cx0, ps_cx1)
                ps_ct = (ps_ct0, ps_ct1)
                ev_kp = [0, 0]
                se_exp = [0, 0]
                ve_cx = [0, 0]
                ve_ct = [0, 0]
                ctx_wr = [0] * NTC
                pending = []
                ev_kp = [(s_ve, 0), (s_ve, 0)]
                for h in range(H):
                    hrow = 64 * (h % 2)
                    hcol = R * (h // 2)

                    def kp_evac(dst_ap, par, idx):
                        if idx % 2 == 0:
                            return (s_ve, E.emit("vector", lambda e, dst_ap=dst_ap, par=par: e.tensor_copy(
                                dst_ap, ps_kp[par][:]), waits=[(s_mm, None)], inc=s_ve))
                        return (s_se, E.emit("scalar", lambda e, dst_ap=dst_ap, par=par: e.activation(
                            dst_ap, ps_kp[par][:], AF.Copy), waits=[(s_mm, None)], inc=s_se))

                    for rc in range(4):
                        for tcol in range(4):
                            idx = rc * 4 + tcol
                            par = idx % 2
                            first = (idx == 0)
                            E.emit("tensor", lambda e, rc=rc, tcol=tcol, par=par, h=h, hrow=hrow, hcol=hcol: e.matmul(
                                ps_kp[par][:],
                                projW[hrow:hrow + 64, (H // 2) * R + hcol + 128 * rc: (H // 2) * R + hcol + 128 * (rc + 1)],
                                kT[hrow:hrow + 64, L * (h // 2) + 512 * tcol: L * (h // 2) + 512 * (tcol + 1)],
                                start=True, stop=True),
                                waits=[(s_ld, ld_proj), ev_kp[par]] +
                                      ([(s_ve, None), (s_se, None)] if first else []), inc=s_mm)
                            ev_kp[par] = kp_evac(kpT[:, L * rc + 512 * tcol: L * rc + 512 * (tcol + 1)], par, idx)
                    if h == 0 and b == 0 and "kp0" in dbg:
                        E.wait_only("sync", [ev_kp[0], ev_kp[1]])
                        E.emit("sync", lambda e: e.dma_start(out=dbg["kp0"], in_=kpT[:]),
                               waits=[(s_ve, None), (s_se, None)], inc=s_st, incn=16)
                    for quad in range(4):
                        for rc in range(4):
                            par = rc % 2
                            E.emit("tensor", lambda e, rc=rc, par=par, h=h, hrow=hrow, hcol=hcol, quad=quad: e.matmul(
                                ps_kp[par][:],
                                projW[hrow:hrow + 64, hcol + 128 * rc: hcol + 128 * (rc + 1)],
                                qT[hrow:hrow + 64, L * (h // 2) + 512 * quad: L * (h // 2) + 512 * (quad + 1)],
                                start=True, stop=True),
                                waits=[ev_kp[par]], inc=s_mm)
                            ev_kp[par] = kp_evac(qpT[:, 512 * rc:512 * (rc + 1)], par, rc)
                        qp_waits = [ev_kp[0], ev_kp[1]]
                        for kc in range(16):
                            par = kc % 2
                            q0 = 512 * quad
                            def mmsc(e, kc=kc, par=par, h=h, quad=quad, q0=q0):
                                lo = max(q0, 128 * kc - 64, 0)
                                hi = min(q0 + 512, 128 * kc + 192, L)
                                have_slab = lo < hi
                                have_c0 = (kc == 0 and quad == 0)
                                have_cL = (kc == 15 and quad == 3)
                                n_extra = int(have_slab) + int(have_c0) + int(have_cL)
                                ins = None
                                for rc in range(4):
                                    ins = e.matmul(ps_s[par][:],
                                                   kpT[:, L * rc + 128 * kc: L * rc + 128 * (kc + 1)],
                                                   qpT[:, 512 * rc:512 * (rc + 1)],
                                                   start=(rc == 0), stop=(rc == 3 and n_extra == 0),
                                                   skip_group_check=True)
                                k = n_extra
                                if have_slab:
                                    k -= 1
                                    ins = e.matmul(ps_s[par][:, lo - q0:hi - q0], identb[:],
                                                   slabs[:, 256 * h + lo - (128 * kc - 64): 256 * h + hi - (128 * kc - 64)],
                                                   start=False, stop=(k == 0), skip_group_check=True)
                                if have_c0:
                                    k -= 1
                                    ins = e.matmul(ps_s[par][0:64, 0:32], identb[0:64, 0:64],
                                                   corr0s[:, 32 * h:32 * (h + 1)],
                                                   start=False, stop=(k == 0), skip_group_check=True)
                                if have_cL:
                                    k -= 1
                                    ins = e.matmul(ps_s[par][64:128, 480:512], identb[0:64, 0:64],
                                                   corrLs[:, 32 * h:32 * (h + 1)],
                                                   start=False, stop=(k == 0), skip_group_check=True)
                                return ins
                            mmsv = E.emit("tensor", mmsc,
                                          waits=(qp_waits if kc == 0 else []) + [(s_se, se_exp[par])], inc=s_mm)
                            se_exp[par] = E.emit("scalar", lambda e, kc=kc, par=par: e.activation(
                                ptS[:, 512 * kc:512 * (kc + 1)], ps_s[par][:], AF.Exp),
                                waits=[(s_mm, mmsv)], inc=s_se)
                            if kc >= 1:
                                kcp = kc - 1
                                E.emit("tensor", lambda e, kcp=kcp, h=h, quad=quad: e.matmul(
                                    ps_cx[quad % 2][:],
                                    vaug[:, 1040 * kcp + 65 * h: 1040 * kcp + 65 * (h + 1)],
                                    ptS[:, 512 * kcp:512 * (kcp + 1)],
                                    start=(kcp == 0), stop=False, skip_group_check=True),
                                    waits=[(s_se, se_exp[kcp % 2])] +
                                          ([(s_ve, ve_cx[quad % 2]), (s_ve, ve_vaug)] if kcp == 0 else []))
                            if kc in (2, 5, 8, 11, 14) and pending:
                                pending.pop(0)()
                        mm_pv_done = E.emit("tensor", lambda e, h=h, quad=quad: e.matmul(
                            ps_cx[quad % 2][:],
                            vaug[:, 1040 * 15 + 65 * h: 1040 * 15 + 65 * (h + 1)],
                            ptS[:, 512 * 15:512 * 16],
                            start=False, stop=True, skip_group_check=True),
                            waits=[(s_se, se_exp[15 % 2])], inc=s_mm)
                        if h == 0 and quad == 0 and b == 0 and "pt00" in dbg:
                            E.emit("sync", lambda e: e.dma_start(out=dbg["pt00"], in_=ptS[:]),
                                   waits=[(s_se, se_exp[1])], inc=s_st, incn=16)

                        def mk_ctx_steps(h=h, quad=quad, mm_pv_done=mm_pv_done):
                            qpar = quad % 2
                            def st_evac():
                                ve_cx[qpar] = E.emit("vector", lambda e: e.tensor_copy(
                                    ctxf[:], ps_cx[qpar][0:65, :]),
                                    waits=[(s_mm, mm_pv_done), (s_st, st_ctx_prev)], inc=s_ve)
                            steps = [st_evac]
                            for j in range(4):
                                def st_j(j=j):
                                    tpar = j % 2
                                    tq_idx = 4 * quad + j
                                    mt2 = E.emit("tensor", lambda e: e.transpose(
                                        ps_ct[tpar][:], ctxf[:, 128 * j:128 * (j + 1)], identf[0:65, 0:65]),
                                        waits=[(s_ve, ve_cx[qpar]), (s_ve, ve_ct[tpar])], inc=s_mm)
                                    E.fenced("vector", lambda e: e.reciprocal(
                                        rs_t[:, tpar:tpar + 1], ps_ct[tpar][:, 64:65]),
                                        s_ve, waits=[(s_mm, mt2)])
                                    ve_ct[tpar] = E.emit("vector", lambda e: e.tensor_scalar(
                                        ctxS[:, DM * tq_idx + 64 * h: DM * tq_idx + 64 * (h + 1)],
                                        ps_ct[tpar][:, 0:64], rs_t[:, tpar:tpar + 1], None, ALU.mult),
                                        inc=s_ve)
                                    ctx_wr[tq_idx] = ve_ct[tpar]
                                steps.append(st_j)
                            return steps
                        pending.extend(mk_ctx_steps())
                for stp in pending:
                    stp()
                pending = []
                st_ctx = [0] * NTC
                for tc in range(NTC):
                    st_ctx[tc] = E.emit("sync", lambda e, tc=tc: e.dma_start(
                        out=ctx_dram[128 * tc:128 * (tc + 1), :], in_=ctxS[:, DM * tc:DM * (tc + 1)]),
                        waits=[(s_ve, ctx_wr[tc]), (s_ldx, st_ctx_rd)], inc=s_st, incn=16)
                st_ctx_prev = s_st._v
                if "ctx" in dbg:
                    E.emit("sync", lambda e: e.dma_start(out=dbg["ctx"], in_=ctxS[:]),
                           waits=[(s_ve, ctx_wr[NTC - 1])], inc=s_st, incn=16)

        with contextlib.ExitStack() as _esd:
            wgT_sb = _esd.enter_context(nc.sbuf_tensor("wgT_sb%d" % b, [128, NKC * IF], BF16))
            wuT_sb = _esd.enter_context(nc.sbuf_tensor("wuT_sb%d" % b, [128, NKC * IF], BF16))
            wdT_sb = _esd.enter_context(nc.sbuf_tensor("wdT_sb%d" % b, [128, NIC * DM], BF16))
            xt2d = _esd.enter_context(nc.sbuf_tensor("xt2d%d" % b, [128, 2 * DM], F32))
            cst1 = _esd.enter_context(nc.sbuf_tensor("cst1%d" % b, [128, DM], BF16))
            x1t2 = _esd.enter_context(nc.sbuf_tensor("x1t2%d" % b, [128, 2 * DM], F32))
            xqf = _esd.enter_context(nc.sbuf_tensor("xqf%d" % b, [128, DM], F32))
            hq_tm = _esd.enter_context(nc.sbuf_tensor("hq_tm%d" % b, [128, IF], BF16))
            xq_tm = _esd.enter_context(nc.sbuf_tensor("xq_tm%d" % b, [128, DM], BF16))
            tT = _esd.enter_context(nc.sbuf_tensor("tT%d" % b, [128, NIC * 128], BF16))
            xqT = tT[:, 0:NKC * 128]
            hqT = tT[:, 0:NIC * 128]
            QTMP["f"] = _esd.enter_context(nc.sbuf_tensor("qdf%d" % b, [128, 3 * 512], F32))
            QTMP["c2"] = _esd.enter_context(nc.sbuf_tensor("qdc2%d" % b, [128, 512], F32))
            QTMP["i"] = _esd.enter_context(nc.sbuf_tensor("qdi%d" % b, [128, 3 * 512], I16))
            QTMP["m"] = _esd.enter_context(nc.sbuf_tensor("qdm%d" % b, [128, 384], F32))
            sgl = tT[:, 1024:2048]
            ps_g = _esd.enter_context(nc.psum_tensor("ps_g%d" % b, [128, 512], F32))
            ps_u = _esd.enter_context(nc.psum_tensor("ps_u%d" % b, [128, 512], F32))
            ps_h = (_esd.enter_context(nc.psum_tensor("ps_h0%d" % b, [128, 512], F32)),
                    _esd.enter_context(nc.psum_tensor("ps_h1%d" % b, [128, 512], F32)))
            ps_dn0 = _esd.enter_context(nc.psum_tensor("ps_dn0%d" % b, [128, 512], F32))
            ps_dn1 = _esd.enter_context(nc.psum_tensor("ps_dn1%d" % b, [128, 512], F32))
            ps_f0 = _esd.enter_context(nc.psum_tensor("ps_f0%d" % b, [128, 128], BF16))
            ps_f1 = _esd.enter_context(nc.psum_tensor("ps_f1%d" % b, [128, 128], BF16))
            ps_f = (ps_f0, ps_f1)
            for kc in range(NKC):
                E.emit("sync", lambda e, kc=kc: e.dma_start(
                    out=wgT_sb[:, IF * kc:IF * (kc + 1)].rearrange("p (r s) -> p r s", r=8),
                    in_=wQ_ld["g"][kc]),
                    waits=[(s_cc, cc_wq_done)], inc=s_ld, incn=16)
            for kc in range(NKC):
                E.emit("sync", lambda e, kc=kc: e.dma_start(
                    out=wuT_sb[:, IF * kc:IF * (kc + 1)].rearrange("p (r s) -> p r s", r=8),
                    in_=wQ_ld["u"][kc]),
                    inc=s_ld, incn=16)
            lwu = s_ld._v
            for kc in range(NIC):
                E.emit("sync", lambda e, kc=kc: e.dma_start(
                    out=wdT_sb[:, DM * kc:DM * (kc + 1)], in_=wQ_ld["d"][128 * kc:128 * (kc + 1), :]),
                    inc=s_ld, incn=16)
            lwd = s_ld._v
            ve_x1_free = 0
            for tc in range(NTC):
                par = tc % 2
                lx = E.emit("sync", lambda e, b=b, tc=tc, par=par: e.dma_start(
                    out=xt2d[:, DM * par:DM * (par + 1)], in_=x_d[b, 128 * tc:128 * (tc + 1), :]),
                    waits=[(s_ve, ve_x1_free)], inc=s_ldx, incn=16)
                lc = E.emit("sync", lambda e, tc=tc: e.dma_start(
                    out=cst1[:], in_=ctx_dram[128 * tc:128 * (tc + 1), :]),
                    waits=[(s_st, st_ctx[tc]), (s_ve, ve_x1_free)], inc=s_ldx, incn=16)
                x1v = E.emit("vector", lambda e, par=par: e.tensor_tensor(
                    x1t2[:, 0:DM], xt2d[:, DM * par:DM * (par + 1)], cst1[:], ALU.add),
                    waits=[(s_ldx, lc), (s_se, None)], inc=s_ve)
                ve_x1_free = emit_ln(x1t2[:, 0:DM], mrow[:, tc:tc + 1], rrow[:, tc:tc + 1],
                                     nbias[:, tc:tc + 1], waits0=[(s_ve, x1v)])

            se_evT = [0, 0]
            st_out_par = [0, 0]
            xq_ready = [0] * NTC
            ve_xq_t = [0] * NTC
            x1v_t = [0] * NTC
            sgv_last = [0]
            hv_last = [0]
            hq_mt_last = [0]
            ve_out_last = [0]
            lc_prev = [s_ve._v]
            ve_pass1 = s_ve._v

            def tile_front(tc):
                par = tc % 2
                lx = E.emit("sync", lambda e, b=b, tc=tc, par=par: e.dma_start(
                    out=xt2d[:, DM * par:DM * (par + 1)], in_=x_d[b, 128 * tc:128 * (tc + 1), :]),
                    waits=[(s_st, st_out_par[par]), (s_ve, x1v_t[tc - 2] if tc >= 2 else ve_pass1)],
                    inc=s_ldx, incn=16)
                lc = E.emit("sync", lambda e, tc=tc: e.dma_start(
                    out=cst1[:], in_=ctx_dram[128 * tc:128 * (tc + 1), :]),
                    waits=[(s_st, st_ctx[tc]), (s_ve, lc_prev[0])], inc=s_ldx, incn=16)
                x1 = x1t2[:, DM * par:DM * (par + 1)]
                x1v = E.emit("vector", lambda e, x1=x1, par=par: e.tensor_tensor(
                    x1, xt2d[:, DM * par:DM * (par + 1)], cst1[:], ALU.add),
                    waits=[(s_ldx, lc), (s_se, None)], inc=s_ve)
                lc_prev[0] = x1v
                x1v_t[tc] = x1v
                xl2 = E.emit("scalar", lambda e, x1=x1, tc=tc: e.activation(
                    xqf[:], x1, AF.Identity, bias=nbias[:, tc:tc + 1], scale=rrow[:, tc:tc + 1]),
                    waits=[(s_ve, x1v)], inc=s_se)
                ve_xq_t[tc] = quant_pair(
                    (xqf[:, 0:512], xq_tm[:, 0:512], 512, QTMP, 0,
                     [(s_se, xl2)], [(s_mm, xq_mt_last[0])]),
                    (xqf[:, 512:1024], xq_tm[:, 512:1024], 512, QTMP, 1,
                     [(s_se, xl2)], [(s_mm, xq_mt_last[0])]))

            xq_mt_last = [0]

            def tile_xqT(tc):
                for c in range(NKC):
                    tpar = c % 2
                    mt3 = E.emit("tensor", lambda e, c=c, tpar=tpar: e.transpose(
                        ps_f[tpar][:], xq_tm[:, 128 * c:128 * (c + 1)], identb[:]),
                        waits=[(s_ve, ve_xq_t[tc]), (s_se, se_evT[tpar])], inc=s_mm)
                    se_evT[tpar] = E.emit("scalar", lambda e, c=c, tpar=tpar: e.activation(
                        xqT[:, 128 * c:128 * (c + 1)], ps_f[tpar][:], AF.Copy),
                        waits=[(s_mm, mt3)], inc=s_se)
                    xq_mt_last[0] = mt3
                xq_ready[tc] = s_se._v

            def tile_matmuls(tc):
                par = tc % 2
                x1 = x1t2[:, DM * par:DM * (par + 1)]
                hq_used = [0, 0]
                for gp in range(0, len(GRPS), 2):
                    pr = []
                    for gi in range(gp, min(gp + 2, len(GRPS))):
                        g0, gw = GRPS[gi]
                        hp = gi % 2
                        def mmg(e, g0=g0, gw=gw):
                            ins = None
                            for kc in range(NKC):
                                ins = e.matmul(ps_g[:, 0:gw], xqT[:, 128 * kc:128 * (kc + 1)],
                                               wgT_sb[:, IF * kc + g0: IF * kc + g0 + gw],
                                               start=(kc == 0), stop=(kc == NKC - 1), skip_group_check=True)
                            return ins
                        def mmu(e, g0=g0, gw=gw):
                            ins = None
                            for kc in range(NKC):
                                ins = e.matmul(ps_u[:, 0:gw], xqT[:, 128 * kc:128 * (kc + 1)],
                                               wuT_sb[:, IF * kc + g0: IF * kc + g0 + gw],
                                               start=(kc == 0), stop=(kc == NKC - 1), skip_group_check=True)
                            return ins
                        mg = E.emit("tensor", mmg,
                                    waits=[(s_se, max(xq_ready[tc], sgv_last[0])), (s_ld, lwu)], inc=s_mm)
                        mu = E.emit("tensor", mmu, waits=[(s_ve, hv_last[0])], inc=s_mm)
                        sgv = E.emit("scalar", lambda e, gw=gw, hp=hp: e.activation(
                            sgl[:, 512 * hp:512 * hp + gw], ps_g[:, 0:gw], AF.Silu),
                            waits=[(s_mm, mg)], inc=s_se)
                        sgv_last[0] = sgv
                        hv = E.emit("vector", lambda e, gw=gw, hp=hp: e.tensor_tensor(
                            ps_h[hp][:, 0:gw], sgl[:, 512 * hp:512 * hp + gw], ps_u[:, 0:gw], ALU.mult),
                            waits=[(s_se, sgv), (s_mm, mu), (s_ve, hq_used[hp])], inc=s_ve)
                        hv_last[0] = hv
                        pr.append((ps_h[hp][:, 0:gw], hq_tm[:, g0:g0 + gw], gw, QTMP, hp,
                                   [(s_ve, hv)], [(s_mm, hq_mt_last[0])]))
                    vq = quant_pair(pr[0], pr[1] if len(pr) > 1 else None)
                    hq_used[0] = hq_used[1] = vq
                for c in range(NIC):
                    tpar = c % 2
                    mt4 = E.emit("tensor", lambda e, c=c, tpar=tpar: e.transpose(
                        ps_f[tpar][:], hq_tm[:, 128 * c:128 * (c + 1)], identb[:]),
                        waits=[(s_ve, None), (s_se, se_evT[tpar])], inc=s_mm)
                    se_evT[tpar] = E.emit("scalar", lambda e, c=c, tpar=tpar: e.activation(
                        hqT[:, 128 * c:128 * (c + 1)], ps_f[tpar][:], AF.Copy),
                        waits=[(s_mm, mt4)], inc=s_se)
                    hq_mt_last[0] = mt4
                se_hqT = s_se._v
                def mmd(e):
                    ins = None
                    for half in range(2):
                        for kc in range(NIC):
                            ins = e.matmul((ps_dn0 if half == 0 else ps_dn1)[:],
                                           hqT[:, 128 * kc:128 * (kc + 1)],
                                           wdT_sb[:, DM * kc + 512 * half: DM * kc + 512 * (half + 1)],
                                           start=(kc == 0), stop=(kc == NIC - 1), skip_group_check=True)
                    return ins
                md = E.emit("tensor", mmd,
                            waits=[(s_se, se_hqT), (s_ld, lwd), (s_ve, ve_out_last[0])], inc=s_mm)
                outb = xt2d[:, DM * par:DM * (par + 1)]
                E.emit("vector", lambda e, x1=x1, outb=outb: e.tensor_tensor(
                    outb[:, 0:512], x1[:, 0:512], ps_dn0[:], ALU.add),
                    waits=[(s_mm, md)], inc=s_ve)
                ov = E.emit("vector", lambda e, x1=x1, outb=outb: e.tensor_tensor(
                    outb[:, 512:1024], x1[:, 512:1024], ps_dn1[:], ALU.add),
                    inc=s_ve)
                ve_out_last[0] = ov
                st_out_par[par] = E.emit("sync", lambda e, b=b, tc=tc, par=par, outb=outb: e.dma_start(
                    out=out_d[b, 128 * tc:128 * (tc + 1), :], in_=outb),
                    waits=[(s_ve, ov)], inc=s_st, incn=16)

            for tc in range(NTC):
                tile_front(tc)
                if tc >= 1:
                    tile_matmuls(tc - 1)
                tile_xqT(tc)
            tile_matmuls(NTC - 1)
            st_ctx_rd = s_ldx._v
        ve_xt_free_start = s_ve._v

    E.wait_only("sync", [(s_st, s_st._v)])

    with nc.Block() as block:
        E.run(block)
    return nc, dbg



from concourse.bass_utils import run_bass_kernel_spmd

_N_CORES = 8
_NB = 16 // _N_CORES


def kernel(L0_attn_hidden_states, L0_attn_Wq, L0_attn_bq, L0_attn_Wk, L0_attn_bk,
           L0_attn_Wv, L0_attn_bv, L0_attn_coefficients, L0_attn_q_proj_weight,
           L0_attn_k_proj_weight, L0_ffn_w_gate, L0_ffn_w_up, L0_ffn_w_down):
    x = np.ascontiguousarray(np.asarray(L0_attn_hidden_states, dtype=np.float32))
    shared = {
        "wq": np.ascontiguousarray(np.asarray(L0_attn_Wq, dtype=np.float32)),
        "bq": np.ascontiguousarray(np.asarray(L0_attn_bq, dtype=np.float32)),
        "wk": np.ascontiguousarray(np.asarray(L0_attn_Wk, dtype=np.float32)),
        "bk": np.ascontiguousarray(np.asarray(L0_attn_bk, dtype=np.float32)),
        "wv": np.ascontiguousarray(np.asarray(L0_attn_Wv, dtype=np.float32)),
        "bv": np.ascontiguousarray(np.asarray(L0_attn_bv, dtype=np.float32)),
        "coef": np.ascontiguousarray(np.asarray(L0_attn_coefficients, dtype=np.float32)),
        "qproj": np.ascontiguousarray(np.asarray(L0_attn_q_proj_weight, dtype=np.float32)),
        "kproj": np.ascontiguousarray(np.asarray(L0_attn_k_proj_weight, dtype=np.float32)),
    }
    shared.update(host_constants())
    wg = np.asarray(L0_ffn_w_gate, dtype=np.float32)
    wu = np.asarray(L0_ffn_w_up, dtype=np.float32)
    wd = np.asarray(L0_ffn_w_down, dtype=np.float32)
    nc, _ = build_core_program(NB=_NB)
    in_maps = []
    for c in range(_N_CORES):
        m = dict(shared)
        m["x"] = np.ascontiguousarray(x[_NB * c:_NB * (c + 1)])
        m["wgsh"] = np.ascontiguousarray(wg[352 * c:352 * (c + 1)])
        m["wush"] = np.ascontiguousarray(wu[352 * c:352 * (c + 1)])
        m["wdsh"] = np.ascontiguousarray(wd[:, 352 * c:352 * (c + 1)])
        in_maps.append(m)
    res = run_bass_kernel_spmd(nc, in_maps, core_ids=list(range(_N_CORES)))
    out = np.concatenate([np.asarray(res.results[c]["out"]) for c in range(_N_CORES)], axis=0)
    return out.astype(np.float32, copy=False)


# version 21
# speedup vs baseline: 1.1361x; 1.1361x over previous
import math
import contextlib
import numpy as np
import ml_dtypes
import concourse.bass as bass
import concourse.mybir as mybir

F32 = mybir.dt.float32
BF16 = mybir.dt.bfloat16
I32 = mybir.dt.int32
I16 = mybir.dt.int16
AF = mybir.ActivationFunctionType
ALU = mybir.AluOpType
AX = mybir.AxisListType

L, DM, H, DH, R, IF = 2048, 1024, 16, 64, 512, 2816
NTC, NKC, NIC = L // 128, DM // 128, IF // 128
GRPS = [(g * 512, 512) for g in range(5)] + [(2560, 256)]
EPS = 1e-5
QSCALE = 1.0 / math.sqrt(R)
ENGINES = ("sync", "tensor", "vector", "scalar", "gpsimd")
QNAMES = ("u", "p1", "c1", "c2", "B", "s1", "s2", "s3")


def host_constants():
    c = {}
    c["ident_f32"] = np.eye(128, dtype=np.float32)
    c["ident_bf16"] = np.eye(128, dtype=ml_dtypes.bfloat16)
    c["one_f32"] = np.ones((1, 1), dtype=np.float32)
    c["onescol_f32"] = np.ones((1, 128), dtype=np.float32)
    crows = np.zeros((128, 2), dtype=np.float32)
    crows[:, 0] = 4.0
    c["crows"] = crows

    def fbasis(t, m):
        s = math.sqrt((1.0 if m == 0 else 2.0) / L)
        return np.cos(np.pi * (np.abs(t) + 0.5) * m / L) * s

    rr = np.arange(128)[:, None]
    cc = np.arange(256)[None, :]
    t = cc - rr - 64
    band = (t >= -31) & (t <= 32)
    Cm = np.stack([fbasis(t, m) * band for m in range(6)])
    c["slab_basis"] = np.ascontiguousarray(Cm.transpose(1, 0, 2).reshape(128, 6 * 256)).astype(np.float32)
    kk = np.arange(64)[:, None]
    qq = np.arange(32)[None, :]
    t0 = kk - qq
    C0 = np.stack([fbasis(t0, m) * (t0 >= 32) for m in range(6)])
    kL = (1984 + np.arange(64))[:, None]
    qL = (2016 + np.arange(32))[None, :]
    tL = qL - kL
    mL = (tL >= 33) & (kL <= 2014)
    CL = np.stack([fbasis(tL, m) * mL for m in range(6)])
    c["corr0_basis"] = np.ascontiguousarray(C0.transpose(1, 0, 2).reshape(64, 6 * 32)).astype(np.float32)
    c["corrL_basis"] = np.ascontiguousarray(CL.transpose(1, 0, 2).reshape(64, 6 * 32)).astype(np.float32)
    return c


class SemW:
    __slots__ = ("h", "_v")
    def __init__(self, h):
        self.h, self._v = h, 0


class Emitter:
    def __init__(self, nc, es):
        self.nc, self.es = nc, es
        self.prog = {e: [] for e in ENGINES}

    def sem(self, name):
        return SemW(self.es.enter_context(self.nc.semaphore(name)))

    def emit(self, eng, fn, waits=(), inc=None, incn=1):
        waits = [(s, (v if v is not None else s._v)) for (s, v) in waits]
        if inc is not None:
            inc._v += incn
        def thunk(e, fn=fn, waits=waits, inc=inc, incn=incn):
            for s, v in waits:
                if v > 0:
                    e.wait_ge(s.h, v)
            ins = fn(e)
            if inc is not None:
                assert ins is not None, "inc requires fn to return its last instruction"
                ins.then_inc(inc.h, incn)
        self.prog[eng].append(thunk)
        return inc._v if inc is not None else None

    def wait_only(self, eng, waits):
        waits = [(s, (v if v is not None else s._v)) for (s, v) in waits]
        def thunk(e, waits=waits):
            for s, v in waits:
                if v > 0:
                    e.wait_ge(s.h, v)
        self.prog[eng].append(thunk)

    def fenced(self, eng, fn, sem, waits=()):
        v = self.emit(eng, fn, waits=waits, inc=sem)
        self.wait_only(eng, [(sem, v)])
        return v

    def run(self, block):
        block.sync(lambda e: [f(e) for f in self.prog["sync"]])
        block.tensor(lambda e: [f(e) for f in self.prog["tensor"]])
        block.vector(lambda e: [f(e) for f in self.prog["vector"]])
        block.scalar(lambda e: [f(e) for f in self.prog["scalar"]])
        block.gpsimd(lambda e: [f(e) for f in self.prog["gpsimd"]])


def build_core_program(NB=2, debug_dump=()):
    nc = bass.Bass("TRN2", target_bir_lowering=False)
    es = contextlib.ExitStack()
    nc._keepalive_es = es

    def din(name, shape, dtype=F32):
        return nc.dram_tensor(name, shape, dtype, kind="ExternalInput").ap()

    x_d = din("x", [NB, L, DM])
    wq_d, bq_d = din("wq", [DM, DM]), din("bq", [DM])
    wk_d, bk_d = din("wk", [DM, DM]), din("bk", [DM])
    wv_d, bv_d = din("wv", [DM, DM]), din("bv", [DM])
    coef_d = din("coef", [H, 6])
    qproj_d, kproj_d = din("qproj", [H, DH, R]), din("kproj", [H, DH, R])
    wgsh_d = din("wgsh", [IF // 8, DM])
    wush_d = din("wush", [IF // 8, DM])
    wdsh_d = din("wdsh", [DM, IF // 8])
    identf_d = din("ident_f32", [128, 128])
    identb_d = din("ident_bf16", [128, 128], BF16)
    onef_d = din("one_f32", [1, 1])
    onescolf_d = din("onescol_f32", [1, 128])
    crows_d = din("crows", [128, 2])
    slabB_d = din("slab_basis", [128, 6 * 256])
    corr0B_d = din("corr0_basis", [64, 6 * 32])
    corrLB_d = din("corrL_basis", [64, 6 * 32])

    out_d = nc.dram_tensor("out", [NB, L, DM], F32, kind="ExternalOutput").ap()

    ctx_dram = nc.dram_tensor("ctx_scr", [L, DM], BF16, kind="Internal").ap()
    SH = IF // 8
    wQl = {}
    wQ_gout = {}
    wQ_ld = {}
    for tag in ("g", "u"):
        wQl[tag] = nc.dram_tensor("wQl_%s" % tag, [DM, SH], BF16, kind="Internal").ap()
        flat = nc.dram_tensor("wQg_%s" % tag, [8 * DM * SH], BF16, kind="Internal").ap()
        wQ_gout[tag] = flat.rearrange("(a s) -> a s", s=SH)
        wQ_ld[tag] = flat.rearrange("(r mc p s) -> mc p r s", r=8, mc=NKC, p=128)
    wQl["d"] = nc.dram_tensor("wQl_d", [SH, DM], BF16, kind="Internal").ap()
    flat_d = nc.dram_tensor("wQg_d", [IF * DM], BF16, kind="Internal").ap()
    wQ_gout["d"] = flat_d.rearrange("(a m) -> a m", m=DM)
    wQ_ld["d"] = flat_d.rearrange("(a m) -> a m", m=DM)

    dbg = {}
    def ddump(name, shape, dtype):
        if name in debug_dump:
            dbg[name] = nc.dram_tensor("dbg_" + name, list(shape), dtype, kind="ExternalOutput").ap()
    ddump("xln", (128, NKC * L), BF16)
    ddump("qT", (128, NKC * L), BF16)
    ddump("kT", (128, NKC * L), BF16)
    ddump("vaug", (128, NTC * 1040), BF16)
    ddump("km0", (128, 2 * L), BF16)
    ddump("pt00", (128, 16 * 512), BF16)
    ddump("slab0", (128, 256), BF16)
    ddump("ctx", (128, NTC * DM), BF16)
    ddump("wgq", (128, NIC * DM), BF16)
    ddump("x1t", (128, DM), F32)
    ddump("hq", (128, IF), BF16)

    E = Emitter(nc, es)
    sb = lambda name, shape, dtype: es.enter_context(nc.sbuf_tensor(name, shape, dtype))

    identb = sb("identb", [128, 128], BF16)
    identf = sb("identf", [128, 128], F32)
    onef = sb("onef", [1, 1], F32)
    onescolb = sb("onescolb", [1, 128], BF16)
    crows = sb("crows_sb", [128, 2], F32)
    slabs = sb("slabs", [128, H * 256], BF16)
    corr0s = sb("corr0s", [64, H * 32], BF16)
    corrLs = sb("corrLs", [64, H * 32], BF16)
    bqT = sb("bqT", [128, NKC], F32)
    bkT = sb("bkT", [128, NKC], F32)
    Nsb = sb("Nsb", [128, (H // 2) * 64], BF16)
    lnst = sb("lnst", [128, 12], F32)
    sqjunk = sb("sqjunk", [128, DM], BF16)
    mrow = sb("mrow", [128, NTC], F32)
    rrow = sb("rrow", [128, NTC], F32)
    nbias = sb("nbias", [128, NTC], F32)

    s_ld = E.sem("s_ld")
    s_cc = E.sem("s_cc")
    s_ldx = E.sem("s_ldx")
    s_mm = E.sem("s_mm")
    s_ve = E.sem("s_ve")
    s_se = E.sem("s_se")
    s_ge = E.sem("s_ge")
    s_st = E.sem("s_st")

    QTMP = {}

    def quant_head(src_ap, Wd, T, mi, waits_v=(), P=128):
        nb = Wd // 16
        v3 = lambda ap: ap.rearrange("p (b k) -> p b k", k=16)
        mT = T["m"]
        bmv = mT[0:P, 192 * mi:192 * mi + nb]
        rsv = mT[0:P, 192 * mi + 64:192 * mi + 64 + nb]
        shv = mT[0:P, 192 * mi + 128:192 * mi + 128 + nb]
        v1 = E.emit("vector", lambda e: e.tensor_reduce(
            bmv, v3(src_ap), AX.X, ALU.max, apply_absolute_value=True),
            waits=list(waits_v) + [(s_ge, T.get("mge", 0)), (s_se, T.get("mse", 0))], inc=s_ve)
        yield
        E.wait_only("vector", [(s_ve, v1)])
        v2 = E.emit("vector", lambda e: e.tensor_scalar(rsv, bmv, 1.0 / 6.0, 1e-12, ALU.mult, ALU.max), inc=s_ve)
        yield
        E.wait_only("vector", [(s_ve, v2)])
        E.emit("vector", lambda e: e.tensor_scalar(shv, rsv, 0.5, None, ALU.mult))
        v3b = E.emit("vector", lambda e: e.reciprocal(bmv, rsv), inc=s_ve)
        yield
        E.wait_only("vector", [(s_ve, v3b)])

    def quant_tail(src_ap, out_ap, Wd, T, mi, waits_q=(), P=128):
        nb = Wd // 16
        v3 = lambda ap: ap.rearrange("p (b k) -> p b k", k=16)
        bc = lambda ap: ap.unsqueeze(2).broadcast_to((P, nb, 16))
        fT, iT, mT = T["f"], T["i"], T["m"]
        sl = lambda j: (T["c2"][0:P, 0:Wd] if j == 3 else fT[0:P, 512 * j:512 * j + Wd])
        il = lambda j: iT[0:P, 512 * j:512 * j + Wd]
        bmv = mT[0:P, 192 * mi:192 * mi + nb]
        shv = mT[0:P, 192 * mi + 128:192 * mi + 128 + nb]
        uv = E.emit("vector", lambda e: e.tensor_tensor(v3(sl(0)), v3(src_ap), bc(bmv), ALU.mult), inc=s_ve)
        sv = E.emit("scalar", lambda e: e.activation(sl(1), sl(0), AF.Relu,
                                                     bias=crows[0:P, 0:1], scale=2.0),
                    waits=[(s_ve, uv), (s_ge, T.get("mge", 0))], inc=s_se)
        T["mse"] = sv
        E.emit("vector", lambda e: e.tensor_scalar(il(1), sl(0), -4.0, 4.0, ALU.max, ALU.min))
        E.emit("vector", lambda e: e.tensor_scalar(il(2), sl(0), -2.0, 2.0, ALU.max, ALU.min))
        E.emit("vector", lambda e: e.tensor_scalar(sl(3), sl(0), -5.0, -4.0, ALU.is_lt, ALU.mult))
        cv = E.emit("vector", lambda e: e.tensor_scalar(sl(2), sl(0), 5.0, 4.0, ALU.is_gt, ALU.mult), inc=s_ve)
        vA = E.emit("vector", lambda e: e.tensor_scalar(il(0), sl(1), 8.0, None, ALU.min),
                    waits=[(s_se, sv)], inc=s_ve)
        gv = E.emit("gpsimd", lambda e: e.tensor_tensor(sl(1), il(1), il(2), ALU.subtract),
                    waits=[(s_ve, vA)], inc=s_ge)
        gv2 = E.emit("gpsimd", lambda e: e.tensor_tensor(sl(3), sl(2), sl(3), ALU.add),
                     waits=[(s_ve, cv)], inc=s_ge)
        T["mge"] = gv2
        E.emit("vector", lambda e: e.scalar_tensor_tensor(sl(0), sl(1), 2.0, il(0), ALU.mult, ALU.add),
               waits=[(s_ge, gv)])
        E.emit("vector", lambda e: e.scalar_tensor_tensor(sl(2), sl(0), -4.0, sl(3), ALU.add, ALU.add),
               waits=[(s_ge, gv2)])
        return E.emit("vector", lambda e: e.tensor_tensor(v3(out_ap), v3(sl(2)), bc(shv), ALU.mult),
                      waits=list(waits_q), inc=s_ve)

    def quant_pair(a, bq):
        P_ = lambda x: x[7] if len(x) > 7 else 128
        gens = [quant_head(x[0], x[2], x[3], x[4], waits_v=x[5], P=P_(x)) for x in (a, bq) if x]
        alive = list(gens)
        while alive:
            for g in list(alive):
                try:
                    next(g)
                except StopIteration:
                    alive.remove(g)
        last = None
        for x in (a, bq):
            if x:
                last = quant_tail(x[0], x[1], x[2], x[3], x[4], waits_q=x[6], P=P_(x))
        return last

    def emit_quant(src_ap, out_ap, Wd, waits_v=(), waits_q=(), T=None):
        if T is None:
            T = QTMP
        return quant_pair((src_ap, out_ap, Wd, T, 0, list(waits_v), list(waits_q)), None)

    def emit_ln(x_ap, m_dst, r_dst, nb_dst, waits0=()):
        E.fenced("vector", lambda e: e.tensor_reduce(lnst[:, 0:1], x_ap, AX.X, ALU.add),
                 s_ve, waits=list(waits0))
        mv = E.emit("vector", lambda e: e.tensor_scalar(m_dst, lnst[:, 0:1], 1.0 / DM, None, ALU.mult), inc=s_ve)
        sqv = E.emit("scalar", lambda e: e.activation(sqjunk[:], x_ap, AF.Square, bias=m_dst,
                                                      scale=-1.0, accum_out=lnst[:, 1:2]),
                     waits=[(s_ve, mv)], inc=s_se)
        vv = E.emit("vector", lambda e: e.tensor_scalar(lnst[:, 2:3], lnst[:, 1:2], 1.0 / DM, EPS, ALU.mult, ALU.add),
                    waits=[(s_se, sqv)], inc=s_ve)
        sq2 = E.emit("scalar", lambda e: e.activation(lnst[:, 3:4], lnst[:, 2:3], AF.Sqrt),
                     waits=[(s_ve, vv)], inc=s_se)
        E.fenced("vector", lambda e: e.reciprocal(lnst[:, 4:5], lnst[:, 3:4]), s_ve, waits=[(s_se, sq2)])
        E.fenced("vector", lambda e: e.tensor_tensor(lnst[:, 5:6], lnst[:, 4:5], lnst[:, 4:5], ALU.mult), s_ve)
        E.fenced("vector", lambda e: e.tensor_tensor(lnst[:, 6:7], lnst[:, 5:6], lnst[:, 2:3], ALU.mult), s_ve)
        E.fenced("vector", lambda e: e.tensor_scalar(lnst[:, 6:7], lnst[:, 6:7], -0.5, 1.5, ALU.mult, ALU.add), s_ve)
        E.fenced("vector", lambda e: e.tensor_tensor(r_dst, lnst[:, 4:5], lnst[:, 6:7], ALU.mult), s_ve)
        E.fenced("vector", lambda e: e.tensor_tensor(lnst[:, 7:8], m_dst, r_dst, ALU.mult), s_ve)
        return E.emit("vector", lambda e: e.tensor_scalar(nb_dst, lnst[:, 7:8], -1.0, None, ALU.mult), inc=s_ve)

    with contextlib.ExitStack() as _es0:
        slabB = _es0.enter_context(nc.sbuf_tensor("slabB", [128, 6 * 256], F32))
        corr0B = _es0.enter_context(nc.sbuf_tensor("corr0B", [64, 6 * 32], F32))
        corrLB = _es0.enter_context(nc.sbuf_tensor("corrLB", [64, 6 * 32], F32))
        brow = _es0.enter_context(nc.sbuf_tensor("brow", [1, 3 * DM], F32))
        coefrow = _es0.enter_context(nc.sbuf_tensor("coefrow", [1, 6 * H], F32))
        coefb = _es0.enter_context(nc.sbuf_tensor("coefb", [128, 6], F32))
        qscale_t = _es0.enter_context(nc.sbuf_tensor("qscale_t", [1, 1], F32))

        onescolf = _es0.enter_context(nc.sbuf_tensor("onescolf", [1, 128], F32))
        slab_acc = _es0.enter_context(nc.sbuf_tensor("slab_acc", [128, 2 * 256], F32))
        wchunk = _es0.enter_context(nc.sbuf_tensor("wchunk", [128, 2 * 3072], F32))
        wqn = _es0.enter_context(nc.sbuf_tensor("wqn", [128, 2 * 3072], BF16))
        wqnT = _es0.enter_context(nc.sbuf_tensor("wqnT", [128, 2 * 3072], BF16))
        ps_b0 = _es0.enter_context(nc.psum_tensor("ps_b0", [128, 16], F32))
        ps_cf = _es0.enter_context(nc.psum_tensor("ps_cf", [128, 6], F32))
        ps_w = (_es0.enter_context(nc.psum_tensor("ps_w0", [128, 128], BF16)),
                _es0.enter_context(nc.psum_tensor("ps_w1", [128, 128], BF16)))
        for dst, src in ((identf, identf_d), (identb, identb_d), (onef, onef_d),
                         (onescolf, onescolf_d), (crows, crows_d), (slabB, slabB_d),
                         (corr0B, corr0B_d), (corrLB, corrLB_d)):
            E.emit("sync", lambda e, d=dst, s=src: e.dma_start(out=d[:], in_=s[:]), inc=s_ld, incn=16)
        E.emit("sync", lambda e: e.dma_start(out=brow[0:1, 0:DM], in_=bq_d.unsqueeze(0)), inc=s_ld, incn=16)
        E.emit("sync", lambda e: e.dma_start(out=brow[0:1, DM:2 * DM], in_=bk_d.unsqueeze(0)), inc=s_ld, incn=16)
        E.emit("sync", lambda e: e.dma_start(out=brow[0:1, 2 * DM:3 * DM], in_=bv_d.unsqueeze(0)), inc=s_ld, incn=16)
        for h in range(H):
            E.emit("sync", lambda e, h=h: e.dma_start(out=coefrow[0:1, 6 * h:6 * h + 6], in_=coef_d[h:h + 1, :]), inc=s_ld, incn=16)
        HCR = (H // 2) * R
        ld_consts = s_ld._v
        E.emit("vector", lambda e: e.tensor_copy(onescolb[:], onescolf[:]), waits=[(s_ld, ld_consts)], inc=s_ve)
        E.emit("vector", lambda e: e.memset(qscale_t[:], QSCALE), inc=s_ve)
        ve_consts = s_ve._v

        for i in range(NKC):
            E.emit("tensor", lambda e, i=i: e.matmul(
                ps_b0[:, i:i + 1], brow[0:1, 128 * i:128 * (i + 1)], qscale_t[:],
                start=True, stop=True), waits=[(s_ld, ld_consts), (s_ve, ve_consts)])
        mm_b = 0
        for i in range(NKC):
            mm_b = E.emit("tensor", lambda e, i=i: e.matmul(
                ps_b0[:, 8 + i:9 + i], brow[0:1, DM + 128 * i:DM + 128 * (i + 1)], onef[:],
                start=True, stop=True), inc=(s_mm if i == NKC - 1 else None))
        E.emit("vector", lambda e: e.tensor_copy(bqT[:], ps_b0[:, 0:8]), waits=[(s_mm, mm_b)])
        ve_bias = E.emit("vector", lambda e: e.tensor_copy(bkT[:], ps_b0[:, 8:16]), inc=s_ve)

        for h in range(H):
            mmv = E.emit("tensor", lambda e, h=h: e.matmul(
                ps_cf[:], onescolf[:], coefrow[0:1, 6 * h:6 * h + 6],
                start=True, stop=True), waits=[(s_ve, None)], inc=s_mm)
            E.fenced("vector", lambda e: e.tensor_copy(coefb[:], ps_cf[:]), s_ve, waits=[(s_mm, mmv)])
            E.emit("vector", lambda e: e.tensor_scalar(
                slab_acc[:, 0:256], slabB[:, 0:256], coefb[:, 0:1], None, ALU.mult))
            for m in range(1, 6):
                last = m == 5
                E.emit("vector", lambda e, m=m, h=h, last=last: e.scalar_tensor_tensor(
                    (slabs[:, 256 * h:256 * (h + 1)] if last else slab_acc[:, 256 * (m % 2):256 * (m % 2) + 256]),
                    slabB[:, 256 * m:256 * (m + 1)], coefb[:, m:m + 1],
                    slab_acc[:, 256 * ((m - 1) % 2):256 * ((m - 1) % 2) + 256],
                    ALU.mult, ALU.add), inc=(s_ve if last else None))
            for Bt, St in ((corr0B, corr0s), (corrLB, corrLs)):
                E.fenced("vector", lambda e, Bt=Bt: e.tensor_scalar(
                    slab_acc[0:64, 0:32], Bt[:, 0:32], coefb[0:64, 0:1], None, ALU.mult), s_ve)
                for m in range(1, 6):
                    last = m == 5
                    E.fenced("vector", lambda e, m=m, h=h, last=last, Bt=Bt, St=St: e.scalar_tensor_tensor(
                        (St[:, 32 * h:32 * (h + 1)] if last else slab_acc[0:64, 32 * (m % 2):32 * (m % 2) + 32]),
                        Bt[:, 32 * m:32 * (m + 1)], coefb[0:64, m:m + 1],
                        slab_acc[0:64, 32 * ((m - 1) % 2):32 * ((m - 1) % 2) + 32],
                        ALU.mult, ALU.add), s_ve)
        ve_slabs = s_ve._v
        if "slab0" in dbg:
            E.emit("sync", lambda e: e.dma_start(out=dbg["slab0"], in_=slabs[:, 0:256]),
                   waits=[(s_ve, ve_slabs)], inc=s_st, incn=16)

        projW0 = _es0.enter_context(nc.sbuf_tensor("projW0", [128, 2 * HCR], BF16))
        pqT0 = _es0.enter_context(nc.sbuf_tensor("pqT0", [128, 256], BF16))
        pkT0 = _es0.enter_context(nc.sbuf_tensor("pkT0", [128, 256], BF16))
        ps_pq = (_es0.enter_context(nc.psum_tensor("ps_pq0", [128, 64], BF16)),
                 _es0.enter_context(nc.psum_tensor("ps_pq1", [128, 64], BF16)))
        ps_N = _es0.enter_context(nc.psum_tensor("ps_N", [128, 64], F32))
        for par_ in (0, 1):
            E.emit("gpsimd", lambda e, par_=par_: e.dma_start(
                out=projW0[64 * par_:64 * par_ + 64, 0:HCR].rearrange("d (c r) -> d c r", c=H // 2),
                in_=qproj_d.rearrange("(c par) d r -> par d c r", par=2)[par_]),
                inc=s_ld, incn=16)
            E.emit("gpsimd", lambda e, par_=par_: e.dma_start(
                out=projW0[64 * par_:64 * par_ + 64, HCR:2 * HCR].rearrange("d (c r) -> d c r", c=H // 2),
                in_=kproj_d.rearrange("(c par) d r -> par d c r", par=2)[par_]),
                inc=s_ld, incn=16)
        ld_pj = s_ld._v
        se_pqe = [0, 0]
        mmN_last = 0
        ve_N = 0
        for h in range(H):
            hrow = 64 * (h % 2)
            hcol = R * (h // 2)
            for rc in range(4):
                for pp, (src0, dstT) in enumerate(((hcol, pqT0), (HCR + hcol, pkT0))):
                    mt = E.emit("tensor", lambda e, hrow=hrow, src0=src0, rc=rc, pp=pp: e.transpose(
                        ps_pq[pp][:, 0:64],
                        projW0[hrow:hrow + 64, src0 + 128 * rc: src0 + 128 * (rc + 1)],
                        identb[hrow:hrow + 64, hrow:hrow + 64]),
                        waits=[(s_ld, ld_pj), (s_ld, ld_consts), (s_se, se_pqe[pp])], inc=s_mm)
                    se_pqe[pp] = E.emit("scalar", lambda e, rc=rc, pp=pp, dstT=dstT: e.activation(
                        dstT[:, 64 * rc:64 * (rc + 1)], ps_pq[pp][:, 0:64], AF.Copy),
                        waits=[(s_mm, mt), (s_mm, mmN_last)], inc=s_se)
            evh = s_se._v
            for rc in range(4):
                mmN_last = E.emit("tensor", lambda e, hrow=hrow, rc=rc: e.matmul(
                    ps_N[hrow:hrow + 64, 0:64], pkT0[:, 64 * rc:64 * (rc + 1)],
                    pqT0[:, 64 * rc:64 * (rc + 1)],
                    start=(rc == 0), stop=(rc == 3)),
                    waits=([(s_se, evh), (s_ve, ve_N)] if rc == 0 else []), inc=s_mm)
            ve_N = E.emit("vector", lambda e, hrow=hrow, h=h: e.tensor_copy(
                Nsb[hrow:hrow + 64, 64 * (h // 2):64 * (h // 2) + 64],
                ps_N[hrow:hrow + 64, 0:64]),
                waits=[(s_mm, mmN_last)], inc=s_ve)

        QT2 = []
        for si in range(2):
            QT2.append({
                "f": _es0.enter_context(nc.sbuf_tensor("q0f%d" % si, [128, 3 * 512], F32)),
                "i": _es0.enter_context(nc.sbuf_tensor("q0i%d" % si, [128, 3 * 512], I16)),
                "c2": _es0.enter_context(nc.sbuf_tensor("q0c2%d" % si, [128, 512], F32)),
                "m": _es0.enter_context(nc.sbuf_tensor("q0m%d" % si, [128, 384], F32)),
            })
        STILES = ((0, 0, 128), (1, 128, 128), (2, 256, 96))
        st_wql = {0: 0, 1: 0}
        ve_wq = {0: ve_consts, 1: ve_consts}
        mm_wqT = {0: 0, 1: 0}
        se_evac = [0, 0]
        for ti, (tag, src_d) in enumerate((("g", wgsh_d), ("u", wush_d), ("d", wdsh_d))):
            par = ti % 2
            base = 3072 * par
            if tag != "d":
                lw = []
                for si, s0, sw in STILES:
                    lw.append(E.emit("sync", lambda e, s=src_d, si=si, s0=s0, sw=sw, base=base: e.dma_start(
                        out=wchunk[0:sw, base + 1024 * si: base + 1024 * si + 1024],
                        in_=s[s0:s0 + sw, :]),
                        waits=[(s_ve, ve_wq[par])], inc=s_ld, incn=16))
                for si, s0, sw in STILES:
                    pr = []
                    for k2, g0 in enumerate((0, 512)):
                        pr.append((wchunk[0:sw, base + 1024 * si + g0: base + 1024 * si + g0 + 512],
                                   wqn[0:sw, base + 1024 * si + g0: base + 1024 * si + g0 + 512], 512,
                                   QT2[k2], 0, [(s_ld, lw[si])], [(s_mm, mm_wqT[par])], sw))
                    ve_wq[par] = quant_pair(pr[0], pr[1])
                ve_q = ve_wq[par]
                for si, s0, sw in STILES:
                    for mc in range(NKC):
                        pp = (si * NKC + mc) % 2
                        mt = E.emit("tensor", lambda e, si=si, sw=sw, mc=mc, pp=pp, base=base: e.transpose(
                            ps_w[pp][:, 0:sw],
                            wqn[0:sw, base + 1024 * si + 128 * mc: base + 1024 * si + 128 * (mc + 1)],
                            identb[0:sw, 0:sw]),
                            waits=[(s_ve, ve_q), (s_se, se_evac[pp]), (s_ld, ld_consts)], inc=s_mm)
                        se_evac[pp] = E.emit("scalar", lambda e, s0=s0, sw=sw, mc=mc, pp=pp, base=base: e.activation(
                            wqnT[:, base + SH * mc + s0: base + SH * mc + s0 + sw],
                            ps_w[pp][:, 0:sw], AF.Copy),
                            waits=[(s_mm, mt), (s_st, st_wql[par])], inc=s_se)
                mm_wqT[par] = s_mm._v
                se_done = s_se._v
                st_wql[par] = E.emit("sync", lambda e, tag=tag, base=base: e.dma_start(
                    out=wQl[tag].rearrange("(mc p) s -> p mc s", p=128),
                    in_=wqnT[:, base: base + 2816].rearrange("p (mc s) -> p mc s", mc=NKC)),
                    waits=[(s_se, se_done)], inc=s_st, incn=16)
            else:
                lwd0 = E.emit("sync", lambda e, s=src_d, base=base: e.dma_start(
                    out=wchunk[:, base: base + 2816].rearrange("p (mc s) -> p mc s", mc=NKC),
                    in_=s.rearrange("(mc p) s -> p mc s", p=128)),
                    waits=[(s_ve, ve_wq[par])], inc=s_ld, incn=16)
                for gi in range(0, NKC, 2):
                    pr = []
                    for k2 in range(2):
                        g0 = SH * (gi + k2)
                        pr.append((wchunk[:, base + g0: base + g0 + SH],
                                   wqn[:, base + g0: base + g0 + SH], SH,
                                   QT2[k2], 0, [(s_ld, lwd0)], [(s_mm, mm_wqT[par])], 128))
                    ve_wq[par] = quant_pair(pr[0], pr[1])
                ve_q = ve_wq[par]
                for mc in range(NKC):
                    for si, s0, sw in STILES:
                        pp = (mc * 3 + si) % 2
                        mt = E.emit("tensor", lambda e, mc=mc, s0=s0, sw=sw, pp=pp, base=base: e.transpose(
                            ps_w[pp][0:sw, :],
                            wqn[:, base + SH * mc + s0: base + SH * mc + s0 + sw],
                            identb[:, :]),
                            waits=[(s_ve, ve_q), (s_se, se_evac[pp]), (s_ld, ld_consts)], inc=s_mm)
                        se_evac[pp] = E.emit("scalar", lambda e, mc=mc, si=si, sw=sw, pp=pp, base=base: e.activation(
                            wqnT[0:sw, base + 1024 * si + 128 * mc: base + 1024 * si + 128 * (mc + 1)],
                            ps_w[pp][0:sw, :], AF.Copy),
                            waits=[(s_mm, mt), (s_st, st_wql[par])], inc=s_se)
                mm_wqT[par] = s_mm._v
                se_done = s_se._v
                for si, s0, sw in STILES:
                    st_wql[par] = E.emit("sync", lambda e, si=si, s0=s0, sw=sw, base=base: e.dma_start(
                        out=wQl["d"][s0:s0 + sw, :],
                        in_=wqnT[0:sw, base + 1024 * si: base + 1024 * si + 1024]),
                        waits=[(s_se, se_done)], inc=s_st, incn=16)
            E.emit("gpsimd", lambda e, tag=tag: e.collective_compute(
                "AllGather", mybir.AluOpType.bypass,
                replica_groups=[list(range(8))],
                ins=[wQl[tag].opt()], outs=[wQ_gout[tag].opt()]),
                waits=[(s_st, st_wql[par])], inc=s_cc, incn=1)
        cc_wq_done = s_cc._v

    ve_xt_free_start = 0
    st_ctx_rd = 0
    st_ctx_prev = 0
    st_out_last = 0

    for b in range(NB):
        st_bgate = s_st._v
        with contextlib.ExitStack() as _esb:
            qT = _esb.enter_context(nc.sbuf_tensor("qT%d" % b, [128, NKC * L], BF16))
            kT = _esb.enter_context(nc.sbuf_tensor("kT%d" % b, [128, NKC * L], BF16))
            vaug = _esb.enter_context(nc.sbuf_tensor("vaug%d" % b, [128, NTC * 1040], BF16))
            xlnT_ready = [0] * NTC
            with contextlib.ExitStack() as _esa:
                xlnT = _esa.enter_context(nc.sbuf_tensor("xlnT%d" % b, [128, NKC * L], BF16))
                wqkv = _esa.enter_context(nc.sbuf_tensor("wqkv%d" % b, [128, NKC * DM], BF16))
                xt2 = _esa.enter_context(nc.sbuf_tensor("xt2%d" % b, [128, 2 * DM], F32))
                xlntile = _esa.enter_context(nc.sbuf_tensor("xlntile%d" % b, [128, DM], BF16))

                bvrow = _esa.enter_context(nc.sbuf_tensor("bvrow%d" % b, [1, DM], BF16))
                ps_a0 = _esa.enter_context(nc.psum_tensor("ps_a0%d" % b, [128, 512], F32))
                ps_a1 = _esa.enter_context(nc.psum_tensor("ps_a1%d" % b, [128, 512], F32))
                ps_t0 = _esa.enter_context(nc.psum_tensor("ps_t0%d" % b, [128, 128], BF16))
                ps_t1 = _esa.enter_context(nc.psum_tensor("ps_t1%d" % b, [128, 128], BF16))
                ps_a = (ps_a0, ps_a1)
                ps_t = (ps_t0, ps_t1)
                ev_tp = [0, 0]
                xt_free = [ve_xt_free_start] * NTC
                se_xln_prev = 0
                for tc in range(NTC):
                    par = tc % 2
                    ldx = E.emit("sync", lambda e, b=b, tc=tc, par=par: e.dma_start(
                        out=xt2[:, DM * par:DM * (par + 1)], in_=x_d[b, 128 * tc:128 * (tc + 1), :]),
                        waits=[(s_ve, xt_free[tc - 2] if tc >= 2 else ve_xt_free_start),
                               (s_se, se_xln_prev), (s_st, st_bgate)], inc=s_ldx, incn=16)
                    xt = xt2[:, DM * par:DM * (par + 1)]
                    nbv = emit_ln(xt, lnst[:, 8:9], lnst[:, 9:10], lnst[:, 10:11],
                                  waits0=[(s_ldx, ldx)])
                    xt_free[tc] = nbv
                    lv = E.emit("scalar", lambda e, xt=xt: e.activation(
                        xlntile[:], xt, AF.Identity, bias=lnst[:, 10:11], scale=lnst[:, 9:10]),
                        waits=[(s_ve, nbv)], inc=s_se)
                    se_xln_prev = lv
                    for c in range(NKC):
                        tpar = c % 2
                        mt = E.emit("tensor", lambda e, c=c, tpar=tpar: e.transpose(
                            ps_t[tpar][:], xlntile[:, 128 * c:128 * (c + 1)], identb[:]),
                            waits=[(s_se, lv), (s_ve, ev_tp[tpar])], inc=s_mm)
                        ev_tp[tpar] = E.emit("vector", lambda e, c=c, tc=tc, tpar=tpar: e.tensor_copy(
                            xlnT[:, L * c + 128 * tc: L * c + 128 * (tc + 1)], ps_t[tpar][:]),
                            waits=[(s_mm, mt)], inc=s_ve)
                    xlnT_ready[tc] = ev_tp[(NKC - 1) % 2]

                if "xln" in dbg:
                    E.emit("sync", lambda e: e.dma_start(out=dbg["xln"], in_=xlnT[:]),
                           waits=[(s_ve, xlnT_ready[NTC - 1])], inc=s_st, incn=16)

                ev_ps_a = [0, 0]
                for (wsrc, bT, scl, dstT, dbgk) in ((wq_d, bqT, QSCALE, qT, "qT"), (wk_d, bkT, 1.0, kT, "kT")):
                    ldw = [0] * NKC
                    for kc in range(NKC):
                        ldw[kc] = E.emit("gpsimd", lambda e, wsrc=wsrc, kc=kc: e.dma_start(
                            out=wqkv[:, DM * kc:DM * (kc + 1)], in_=wsrc[128 * kc:128 * (kc + 1), :]),
                            waits=([(s_mm, None), (s_st, st_bgate)] if kc == 0 else []) + [(s_se, ev_ps_a[0]), (s_se, ev_ps_a[1])],
                            inc=s_ld, incn=16)
                    for nci in range(NKC):
                        for tcol in range(4):
                            par = (nci * 4 + tcol) % 2
                            def mmq(e, nci=nci, tcol=tcol, par=par):
                                ins = None
                                for kc in range(NKC):
                                    ins = e.matmul(ps_a[par][:],
                                                   wqkv[:, DM * kc + 128 * nci:DM * kc + 128 * (nci + 1)],
                                                   xlnT[:, L * kc + 512 * tcol:L * kc + 512 * (tcol + 1)],
                                                   start=(kc == 0), stop=(kc == NKC - 1))
                                return ins
                            mmv = E.emit("tensor", mmq,
                                         waits=[(s_ld, ldw[NKC - 1]), (s_ve, xlnT_ready[4 * tcol + 3]),
                                                (s_se, ev_ps_a[par])], inc=s_mm)
                            ev_ps_a[par] = E.emit("scalar", lambda e, nci=nci, tcol=tcol, par=par, bT=bT, scl=scl, dstT=dstT: e.activation(
                                dstT[:, L * nci + 512 * tcol:L * nci + 512 * (tcol + 1)], ps_a[par][:],
                                AF.Identity, bias=bT[:, nci:nci + 1], scale=scl),
                                waits=[(s_mm, mmv)], inc=s_se)
                    if dbgk in dbg:
                        E.emit("sync", lambda e, dstT=dstT, dbgk=dbgk: e.dma_start(out=dbg[dbgk], in_=dstT[:]),
                               waits=[(s_se, None)], inc=s_st, incn=16)

                ldw = [0] * NKC
                for kc in range(NKC):
                    ldw[kc] = E.emit("gpsimd", lambda e, kc=kc: e.dma_start(
                        out=wqkv[:, DM * kc:DM * (kc + 1)], in_=wv_d[128 * kc:128 * (kc + 1), :]),
                        waits=([(s_mm, None), (s_st, st_bgate)] if kc == 0 else []) + [(s_se, ev_ps_a[0]), (s_se, ev_ps_a[1])],
                        inc=s_ld, incn=16)
                ld_bv = E.emit("gpsimd", lambda e: e.dma_start(
                    out=bvrow[:], in_=bv_d.unsqueeze(0)), waits=[(s_st, st_bgate)], inc=s_ld, incn=16)
                E.emit("vector", lambda e: e.memset(
                    vaug[:].rearrange("p (t h d) -> p t h d", t=NTC, d=65)[:, :, :, 64:65], 1.0),
                    waits=[(s_mm, None)], inc=s_ve)
                ev_ps_v = [0, 0]
                for tc in range(NTC):
                    for nh in range(2):
                        par = (tc * 2 + nh) % 2
                        def mmv_(e, tc=tc, nh=nh, par=par):
                            for kc in range(NKC):
                                e.matmul(ps_a[par][:],
                                         xlnT[:, L * kc + 128 * tc:L * kc + 128 * (tc + 1)],
                                         wqkv[:, DM * kc + 512 * nh:DM * kc + 512 * (nh + 1)],
                                         start=(kc == 0), stop=False)
                            return e.matmul(ps_a[par][:], onescolb[:], bvrow[0:1, 512 * nh:512 * (nh + 1)],
                                            start=False, stop=True)
                        mmv2 = E.emit("tensor", mmv_,
                                      waits=[(s_ld, ldw[NKC - 1]), (s_ve, ev_ps_v[par]),
                                             (s_se, None), (s_ve, None)], inc=s_mm)
                        dst = vaug[:, 1040 * tc + 520 * nh: 1040 * tc + 520 * (nh + 1)]
                        ev_ps_v[par] = E.emit("vector", lambda e, dst=dst, par=par: e.tensor_copy(
                            dst.rearrange("p (h d) -> p h d", d=65)[:, :, 0:64],
                            ps_a[par][:].rearrange("p (h d) -> p h d", d=64)),
                            waits=[(s_mm, mmv2)], inc=s_ve)
                ve_vaug = s_ve._v
                if "vaug" in dbg:
                    E.emit("sync", lambda e: e.dma_start(out=dbg["vaug"], in_=vaug[:]),
                           waits=[(s_ve, ve_vaug)], inc=s_st, incn=16)

            with contextlib.ExitStack() as _esk:
                kmT_sb = _esk.enter_context(nc.sbuf_tensor("kmT%d" % b, [128, 2 * L], BF16))
                ptS = _esk.enter_context(nc.sbuf_tensor("ptS%d" % b, [128, 16 * 512], BF16))

                ctxS = _esk.enter_context(nc.sbuf_tensor("ctxS%d" % b, [128, NTC * DM], BF16))

                ctxf = _esk.enter_context(nc.sbuf_tensor("ctxf%d" % b, [65, 512], F32))

                rs_t = _esk.enter_context(nc.sbuf_tensor("rs_t%d" % b, [128, 2], F32))
                ps_kp0 = _esk.enter_context(nc.psum_tensor("ps_kp0%d" % b, [128, 512], F32))
                ps_kp1 = _esk.enter_context(nc.psum_tensor("ps_kp1%d" % b, [128, 512], F32))
                ps_s0 = _esk.enter_context(nc.psum_tensor("ps_s0%d" % b, [128, 512], F32))
                ps_s1 = _esk.enter_context(nc.psum_tensor("ps_s1%d" % b, [128, 512], F32))
                ps_cx0 = _esk.enter_context(nc.psum_tensor("ps_cx0%d" % b, [65, 512], F32))
                ps_cx1 = _esk.enter_context(nc.psum_tensor("ps_cx1%d" % b, [65, 512], F32))
                ps_ct0 = _esk.enter_context(nc.psum_tensor("ps_ct0%d" % b, [128, 65], F32))
                ps_ct1 = _esk.enter_context(nc.psum_tensor("ps_ct1%d" % b, [128, 65], F32))
                ps_kp = (ps_kp0, ps_kp1)
                ps_s = (ps_s0, ps_s1)
                ps_cx = (ps_cx0, ps_cx1)
                ps_ct = (ps_ct0, ps_ct1)
                se_exp = [0, 0]
                ve_cx = [0, 0]
                ve_ct = [0, 0]
                ctx_wr = [0] * NTC
                pending = []
                kp_ev = [(s_ve, 0), (s_se, 0)]
                kmT_ready = {}

                def emit_kmT(h, first=False):
                    hrow = 64 * (h % 2)
                    c = h // 2
                    cblk = c % 2
                    evs = []
                    for chunk in range(4):
                        par = chunk % 2
                        w = [kp_ev[par]]
                        if first and chunk < 2:
                            w += [(s_ve, None), (s_se, None)]
                        mmk = E.emit("tensor", lambda e, hrow=hrow, c=c, chunk=chunk, par=par: e.matmul(
                            ps_kp[par][hrow:hrow + 64, :],
                            Nsb[hrow:hrow + 64, 64 * c:64 * (c + 1)],
                            kT[hrow:hrow + 64, L * c + 512 * chunk: L * c + 512 * (chunk + 1)],
                            start=True, stop=True), waits=w, inc=s_mm)
                        dst = kmT_sb[hrow:hrow + 64, 2048 * cblk + 512 * chunk: 2048 * cblk + 512 * (chunk + 1)]
                        if chunk % 2 == 0:
                            kp_ev[par] = (s_ve, E.emit("vector", lambda e, dst=dst, par=par, hrow=hrow: e.tensor_copy(
                                dst, ps_kp[par][hrow:hrow + 64, :]), waits=[(s_mm, mmk)], inc=s_ve))
                        else:
                            kp_ev[par] = (s_se, E.emit("scalar", lambda e, dst=dst, par=par, hrow=hrow: e.activation(
                                dst, ps_kp[par][hrow:hrow + 64, :], AF.Copy), waits=[(s_mm, mmk)], inc=s_se))
                        evs.append(kp_ev[par])
                    kmT_ready[h] = evs

                emit_kmT(0, first=True)
                emit_kmT(1)
                for h in range(H):
                    hrow = 64 * (h % 2)
                    c = h // 2
                    cblk = c % 2
                    for quad in range(4):
                        km_waits = kmT_ready[h] if quad == 0 else []
                        for kc in range(16):
                            par = kc % 2
                            q0 = 512 * quad
                            def mmsc(e, kc=kc, par=par, h=h, quad=quad, q0=q0, hrow=hrow, c=c, cblk=cblk):
                                lo = max(q0, 128 * kc - 64, 0)
                                hi = min(q0 + 512, 128 * kc + 192, L)
                                have_slab = lo < hi
                                have_c0 = (kc == 0 and quad == 0)
                                have_cL = (kc == 15 and quad == 3)
                                n_extra = int(have_slab) + int(have_c0) + int(have_cL)
                                ins = e.matmul(ps_s[par][:],
                                               kmT_sb[hrow:hrow + 64, 2048 * cblk + 128 * kc: 2048 * cblk + 128 * (kc + 1)],
                                               qT[hrow:hrow + 64, L * c + q0: L * c + q0 + 512],
                                               start=True, stop=(n_extra == 0),
                                               skip_group_check=True)
                                k = n_extra
                                if have_slab:
                                    k -= 1
                                    ins = e.matmul(ps_s[par][:, lo - q0:hi - q0], identb[:],
                                                   slabs[:, 256 * h + lo - (128 * kc - 64): 256 * h + hi - (128 * kc - 64)],
                                                   start=False, stop=(k == 0), skip_group_check=True)
                                if have_c0:
                                    k -= 1
                                    ins = e.matmul(ps_s[par][0:64, 0:32], identb[0:64, 0:64],
                                                   corr0s[:, 32 * h:32 * (h + 1)],
                                                   start=False, stop=(k == 0), skip_group_check=True)
                                if have_cL:
                                    k -= 1
                                    ins = e.matmul(ps_s[par][64:128, 480:512], identb[0:64, 0:64],
                                                   corrLs[:, 32 * h:32 * (h + 1)],
                                                   start=False, stop=(k == 0), skip_group_check=True)
                                return ins
                            mmsv = E.emit("tensor", mmsc,
                                          waits=(km_waits if kc == 0 else []) + [(s_se, se_exp[par])], inc=s_mm)
                            se_exp[par] = E.emit("scalar", lambda e, kc=kc, par=par: e.activation(
                                ptS[:, 512 * kc:512 * (kc + 1)], ps_s[par][:], AF.Exp),
                                waits=[(s_mm, mmsv)], inc=s_se)
                            if kc >= 1:
                                kcp = kc - 1
                                E.emit("tensor", lambda e, kcp=kcp, h=h, quad=quad: e.matmul(
                                    ps_cx[quad % 2][:],
                                    vaug[:, 1040 * kcp + 65 * h: 1040 * kcp + 65 * (h + 1)],
                                    ptS[:, 512 * kcp:512 * (kcp + 1)],
                                    start=(kcp == 0), stop=False, skip_group_check=True),
                                    waits=[(s_se, se_exp[kcp % 2])] +
                                          ([(s_ve, ve_cx[quad % 2]), (s_ve, ve_vaug)] if kcp == 0 else []))
                            if kc in (2, 5, 8, 11, 14) and pending:
                                pending.pop(0)()
                        mm_pv_done = E.emit("tensor", lambda e, h=h, quad=quad: e.matmul(
                            ps_cx[quad % 2][:],
                            vaug[:, 1040 * 15 + 65 * h: 1040 * 15 + 65 * (h + 1)],
                            ptS[:, 512 * 15:512 * 16],
                            start=False, stop=True, skip_group_check=True),
                            waits=[(s_se, se_exp[15 % 2])], inc=s_mm)
                        if h == 0 and quad == 0 and b == 0 and "pt00" in dbg:
                            E.emit("sync", lambda e: e.dma_start(out=dbg["pt00"], in_=ptS[:]),
                                   waits=[(s_se, se_exp[1])], inc=s_st, incn=16)

                        def mk_ctx_steps(h=h, quad=quad, mm_pv_done=mm_pv_done):
                            qpar = quad % 2
                            def st_evac():
                                ve_cx[qpar] = E.emit("vector", lambda e: e.tensor_copy(
                                    ctxf[:], ps_cx[qpar][0:65, :]),
                                    waits=[(s_mm, mm_pv_done), (s_st, st_ctx_prev)], inc=s_ve)
                            steps = [st_evac]
                            for j in range(4):
                                def st_j(j=j):
                                    tpar = j % 2
                                    tq_idx = 4 * quad + j
                                    mt2 = E.emit("tensor", lambda e: e.transpose(
                                        ps_ct[tpar][:], ctxf[:, 128 * j:128 * (j + 1)], identf[0:65, 0:65]),
                                        waits=[(s_ve, ve_cx[qpar]), (s_ve, ve_ct[tpar])], inc=s_mm)
                                    E.fenced("vector", lambda e: e.reciprocal(
                                        rs_t[:, tpar:tpar + 1], ps_ct[tpar][:, 64:65]),
                                        s_ve, waits=[(s_mm, mt2)])
                                    ve_ct[tpar] = E.emit("vector", lambda e: e.tensor_scalar(
                                        ctxS[:, DM * tq_idx + 64 * h: DM * tq_idx + 64 * (h + 1)],
                                        ps_ct[tpar][:, 0:64], rs_t[:, tpar:tpar + 1], None, ALU.mult),
                                        inc=s_ve)
                                    ctx_wr[tq_idx] = ve_ct[tpar]
                                steps.append(st_j)
                            return steps
                        pending.extend(mk_ctx_steps())
                        if quad == 0 and h + 2 < H:
                            emit_kmT(h + 2)
                for stp in pending:
                    stp()
                pending = []
                st_ctx = [0] * NTC
                for tc in range(NTC):
                    st_ctx[tc] = E.emit("sync", lambda e, tc=tc: e.dma_start(
                        out=ctx_dram[128 * tc:128 * (tc + 1), :], in_=ctxS[:, DM * tc:DM * (tc + 1)]),
                        waits=[(s_ve, ctx_wr[tc]), (s_ldx, st_ctx_rd)], inc=s_st, incn=16)
                st_ctx_prev = s_st._v
                if "ctx" in dbg:
                    E.emit("sync", lambda e: e.dma_start(out=dbg["ctx"], in_=ctxS[:]),
                           waits=[(s_ve, ctx_wr[NTC - 1])], inc=s_st, incn=16)

        with contextlib.ExitStack() as _esd:
            wgT_sb = _esd.enter_context(nc.sbuf_tensor("wgT_sb%d" % b, [128, NKC * IF], BF16))
            wuT_sb = _esd.enter_context(nc.sbuf_tensor("wuT_sb%d" % b, [128, NKC * IF], BF16))
            wdT_sb = _esd.enter_context(nc.sbuf_tensor("wdT_sb%d" % b, [128, NIC * DM], BF16))
            xt2d = _esd.enter_context(nc.sbuf_tensor("xt2d%d" % b, [128, 2 * DM], F32))
            cst1 = _esd.enter_context(nc.sbuf_tensor("cst1%d" % b, [128, DM], BF16))
            x1t2 = _esd.enter_context(nc.sbuf_tensor("x1t2%d" % b, [128, 2 * DM], F32))
            xqf = _esd.enter_context(nc.sbuf_tensor("xqf%d" % b, [128, DM], F32))
            hq_tm = _esd.enter_context(nc.sbuf_tensor("hq_tm%d" % b, [128, IF], BF16))
            xq_tm = _esd.enter_context(nc.sbuf_tensor("xq_tm%d" % b, [128, DM], BF16))
            tT = _esd.enter_context(nc.sbuf_tensor("tT%d" % b, [128, NIC * 128], BF16))
            xqT = tT[:, 0:NKC * 128]
            hqT = tT[:, 0:NIC * 128]
            QTMP["f"] = _esd.enter_context(nc.sbuf_tensor("qdf%d" % b, [128, 3 * 512], F32))
            QTMP["c2"] = _esd.enter_context(nc.sbuf_tensor("qdc2%d" % b, [128, 512], F32))
            QTMP["i"] = _esd.enter_context(nc.sbuf_tensor("qdi%d" % b, [128, 3 * 512], I16))
            QTMP["m"] = _esd.enter_context(nc.sbuf_tensor("qdm%d" % b, [128, 384], F32))
            sgl = tT[:, 1024:2048]
            ps_g = _esd.enter_context(nc.psum_tensor("ps_g%d" % b, [128, 512], F32))
            ps_u = _esd.enter_context(nc.psum_tensor("ps_u%d" % b, [128, 512], F32))
            ps_h = (_esd.enter_context(nc.psum_tensor("ps_h0%d" % b, [128, 512], F32)),
                    _esd.enter_context(nc.psum_tensor("ps_h1%d" % b, [128, 512], F32)))
            ps_dn0 = _esd.enter_context(nc.psum_tensor("ps_dn0%d" % b, [128, 512], F32))
            ps_dn1 = _esd.enter_context(nc.psum_tensor("ps_dn1%d" % b, [128, 512], F32))
            ps_f0 = _esd.enter_context(nc.psum_tensor("ps_f0%d" % b, [128, 128], BF16))
            ps_f1 = _esd.enter_context(nc.psum_tensor("ps_f1%d" % b, [128, 128], BF16))
            ps_f = (ps_f0, ps_f1)
            for kc in range(NKC):
                E.emit("sync", lambda e, kc=kc: e.dma_start(
                    out=wgT_sb[:, IF * kc:IF * (kc + 1)].rearrange("p (r s) -> p r s", r=8),
                    in_=wQ_ld["g"][kc]),
                    waits=[(s_cc, cc_wq_done)], inc=s_ld, incn=16)
            for kc in range(NKC):
                E.emit("sync", lambda e, kc=kc: e.dma_start(
                    out=wuT_sb[:, IF * kc:IF * (kc + 1)].rearrange("p (r s) -> p r s", r=8),
                    in_=wQ_ld["u"][kc]),
                    inc=s_ld, incn=16)
            lwu = s_ld._v
            for kc in range(NIC):
                E.emit("sync", lambda e, kc=kc: e.dma_start(
                    out=wdT_sb[:, DM * kc:DM * (kc + 1)], in_=wQ_ld["d"][128 * kc:128 * (kc + 1), :]),
                    inc=s_ld, incn=16)
            lwd = s_ld._v
            ve_x1_free = 0
            for tc in range(NTC):
                par = tc % 2
                lx = E.emit("sync", lambda e, b=b, tc=tc, par=par: e.dma_start(
                    out=xt2d[:, DM * par:DM * (par + 1)], in_=x_d[b, 128 * tc:128 * (tc + 1), :]),
                    waits=[(s_ve, ve_x1_free)], inc=s_ldx, incn=16)
                lc = E.emit("sync", lambda e, tc=tc: e.dma_start(
                    out=cst1[:], in_=ctx_dram[128 * tc:128 * (tc + 1), :]),
                    waits=[(s_st, st_ctx[tc]), (s_ve, ve_x1_free)], inc=s_ldx, incn=16)
                x1v = E.emit("vector", lambda e, par=par: e.tensor_tensor(
                    x1t2[:, 0:DM], xt2d[:, DM * par:DM * (par + 1)], cst1[:], ALU.add),
                    waits=[(s_ldx, lc), (s_se, None)], inc=s_ve)
                ve_x1_free = emit_ln(x1t2[:, 0:DM], mrow[:, tc:tc + 1], rrow[:, tc:tc + 1],
                                     nbias[:, tc:tc + 1], waits0=[(s_ve, x1v)])

            se_evT = [0, 0]
            st_out_par = [0, 0]
            xq_ready = [0] * NTC
            ve_xq_t = [0] * NTC
            x1v_t = [0] * NTC
            sgv_last = [0]
            hv_last = [0]
            hq_mt_last = [0]
            ve_out_last = [0]
            lc_prev = [s_ve._v]
            ve_pass1 = s_ve._v

            def tile_front(tc):
                par = tc % 2
                lx = E.emit("sync", lambda e, b=b, tc=tc, par=par: e.dma_start(
                    out=xt2d[:, DM * par:DM * (par + 1)], in_=x_d[b, 128 * tc:128 * (tc + 1), :]),
                    waits=[(s_st, st_out_par[par]), (s_ve, x1v_t[tc - 2] if tc >= 2 else ve_pass1)],
                    inc=s_ldx, incn=16)
                lc = E.emit("sync", lambda e, tc=tc: e.dma_start(
                    out=cst1[:], in_=ctx_dram[128 * tc:128 * (tc + 1), :]),
                    waits=[(s_st, st_ctx[tc]), (s_ve, lc_prev[0])], inc=s_ldx, incn=16)
                x1 = x1t2[:, DM * par:DM * (par + 1)]
                x1v = E.emit("vector", lambda e, x1=x1, par=par: e.tensor_tensor(
                    x1, xt2d[:, DM * par:DM * (par + 1)], cst1[:], ALU.add),
                    waits=[(s_ldx, lc), (s_se, None)], inc=s_ve)
                lc_prev[0] = x1v
                x1v_t[tc] = x1v
                xl2 = E.emit("scalar", lambda e, x1=x1, tc=tc: e.activation(
                    xqf[:], x1, AF.Identity, bias=nbias[:, tc:tc + 1], scale=rrow[:, tc:tc + 1]),
                    waits=[(s_ve, x1v)], inc=s_se)
                ve_xq_t[tc] = quant_pair(
                    (xqf[:, 0:512], xq_tm[:, 0:512], 512, QTMP, 0,
                     [(s_se, xl2)], [(s_mm, xq_mt_last[0])]),
                    (xqf[:, 512:1024], xq_tm[:, 512:1024], 512, QTMP, 1,
                     [(s_se, xl2)], [(s_mm, xq_mt_last[0])]))

            xq_mt_last = [0]

            def tile_xqT(tc):
                for c in range(NKC):
                    tpar = c % 2
                    mt3 = E.emit("tensor", lambda e, c=c, tpar=tpar: e.transpose(
                        ps_f[tpar][:], xq_tm[:, 128 * c:128 * (c + 1)], identb[:]),
                        waits=[(s_ve, ve_xq_t[tc]), (s_se, se_evT[tpar])], inc=s_mm)
                    se_evT[tpar] = E.emit("scalar", lambda e, c=c, tpar=tpar: e.activation(
                        xqT[:, 128 * c:128 * (c + 1)], ps_f[tpar][:], AF.Copy),
                        waits=[(s_mm, mt3)], inc=s_se)
                    xq_mt_last[0] = mt3
                xq_ready[tc] = s_se._v

            def tile_matmuls(tc):
                par = tc % 2
                x1 = x1t2[:, DM * par:DM * (par + 1)]
                hq_used = [0, 0]
                for gp in range(0, len(GRPS), 2):
                    pr = []
                    for gi in range(gp, min(gp + 2, len(GRPS))):
                        g0, gw = GRPS[gi]
                        hp = gi % 2
                        def mmg(e, g0=g0, gw=gw):
                            ins = None
                            for kc in range(NKC):
                                ins = e.matmul(ps_g[:, 0:gw], xqT[:, 128 * kc:128 * (kc + 1)],
                                               wgT_sb[:, IF * kc + g0: IF * kc + g0 + gw],
                                               start=(kc == 0), stop=(kc == NKC - 1), skip_group_check=True)
                            return ins
                        def mmu(e, g0=g0, gw=gw):
                            ins = None
                            for kc in range(NKC):
                                ins = e.matmul(ps_u[:, 0:gw], xqT[:, 128 * kc:128 * (kc + 1)],
                                               wuT_sb[:, IF * kc + g0: IF * kc + g0 + gw],
                                               start=(kc == 0), stop=(kc == NKC - 1), skip_group_check=True)
                            return ins
                        mg = E.emit("tensor", mmg,
                                    waits=[(s_se, max(xq_ready[tc], sgv_last[0])), (s_ld, lwu)], inc=s_mm)
                        mu = E.emit("tensor", mmu, waits=[(s_ve, hv_last[0])], inc=s_mm)
                        sgv = E.emit("scalar", lambda e, gw=gw, hp=hp: e.activation(
                            sgl[:, 512 * hp:512 * hp + gw], ps_g[:, 0:gw], AF.Silu),
                            waits=[(s_mm, mg)], inc=s_se)
                        sgv_last[0] = sgv
                        hv = E.emit("vector", lambda e, gw=gw, hp=hp: e.tensor_tensor(
                            ps_h[hp][:, 0:gw], sgl[:, 512 * hp:512 * hp + gw], ps_u[:, 0:gw], ALU.mult),
                            waits=[(s_se, sgv), (s_mm, mu), (s_ve, hq_used[hp])], inc=s_ve)
                        hv_last[0] = hv
                        pr.append((ps_h[hp][:, 0:gw], hq_tm[:, g0:g0 + gw], gw, QTMP, hp,
                                   [(s_ve, hv)], [(s_mm, hq_mt_last[0])]))
                    vq = quant_pair(pr[0], pr[1] if len(pr) > 1 else None)
                    hq_used[0] = hq_used[1] = vq
                for c in range(NIC):
                    tpar = c % 2
                    mt4 = E.emit("tensor", lambda e, c=c, tpar=tpar: e.transpose(
                        ps_f[tpar][:], hq_tm[:, 128 * c:128 * (c + 1)], identb[:]),
                        waits=[(s_ve, None), (s_se, se_evT[tpar])], inc=s_mm)
                    se_evT[tpar] = E.emit("scalar", lambda e, c=c, tpar=tpar: e.activation(
                        hqT[:, 128 * c:128 * (c + 1)], ps_f[tpar][:], AF.Copy),
                        waits=[(s_mm, mt4)], inc=s_se)
                    hq_mt_last[0] = mt4
                se_hqT = s_se._v
                def mmd(e):
                    ins = None
                    for half in range(2):
                        for kc in range(NIC):
                            ins = e.matmul((ps_dn0 if half == 0 else ps_dn1)[:],
                                           hqT[:, 128 * kc:128 * (kc + 1)],
                                           wdT_sb[:, DM * kc + 512 * half: DM * kc + 512 * (half + 1)],
                                           start=(kc == 0), stop=(kc == NIC - 1), skip_group_check=True)
                    return ins
                md = E.emit("tensor", mmd,
                            waits=[(s_se, se_hqT), (s_ld, lwd), (s_ve, ve_out_last[0])], inc=s_mm)
                outb = xt2d[:, DM * par:DM * (par + 1)]
                E.emit("vector", lambda e, x1=x1, outb=outb: e.tensor_tensor(
                    outb[:, 0:512], x1[:, 0:512], ps_dn0[:], ALU.add),
                    waits=[(s_mm, md)], inc=s_ve)
                ov = E.emit("vector", lambda e, x1=x1, outb=outb: e.tensor_tensor(
                    outb[:, 512:1024], x1[:, 512:1024], ps_dn1[:], ALU.add),
                    inc=s_ve)
                ve_out_last[0] = ov
                st_out_par[par] = E.emit("sync", lambda e, b=b, tc=tc, par=par, outb=outb: e.dma_start(
                    out=out_d[b, 128 * tc:128 * (tc + 1), :], in_=outb),
                    waits=[(s_ve, ov)], inc=s_st, incn=16)

            for tc in range(NTC):
                tile_front(tc)
                if tc >= 1:
                    tile_matmuls(tc - 1)
                tile_xqT(tc)
            tile_matmuls(NTC - 1)
            st_ctx_rd = s_ldx._v
        ve_xt_free_start = s_ve._v

    E.wait_only("sync", [(s_st, s_st._v)])

    with nc.Block() as block:
        E.run(block)
    return nc, dbg



from concourse.bass_utils import run_bass_kernel_spmd

_N_CORES = 8
_NB = 16 // _N_CORES


def kernel(L0_attn_hidden_states, L0_attn_Wq, L0_attn_bq, L0_attn_Wk, L0_attn_bk,
           L0_attn_Wv, L0_attn_bv, L0_attn_coefficients, L0_attn_q_proj_weight,
           L0_attn_k_proj_weight, L0_ffn_w_gate, L0_ffn_w_up, L0_ffn_w_down):
    x = np.ascontiguousarray(np.asarray(L0_attn_hidden_states, dtype=np.float32))
    shared = {
        "wq": np.ascontiguousarray(np.asarray(L0_attn_Wq, dtype=np.float32)),
        "bq": np.ascontiguousarray(np.asarray(L0_attn_bq, dtype=np.float32)),
        "wk": np.ascontiguousarray(np.asarray(L0_attn_Wk, dtype=np.float32)),
        "bk": np.ascontiguousarray(np.asarray(L0_attn_bk, dtype=np.float32)),
        "wv": np.ascontiguousarray(np.asarray(L0_attn_Wv, dtype=np.float32)),
        "bv": np.ascontiguousarray(np.asarray(L0_attn_bv, dtype=np.float32)),
        "coef": np.ascontiguousarray(np.asarray(L0_attn_coefficients, dtype=np.float32)),
        "qproj": np.ascontiguousarray(np.asarray(L0_attn_q_proj_weight, dtype=np.float32)),
        "kproj": np.ascontiguousarray(np.asarray(L0_attn_k_proj_weight, dtype=np.float32)),
    }
    shared.update(host_constants())
    wg = np.asarray(L0_ffn_w_gate, dtype=np.float32)
    wu = np.asarray(L0_ffn_w_up, dtype=np.float32)
    wd = np.asarray(L0_ffn_w_down, dtype=np.float32)
    nc, _ = build_core_program(NB=_NB)
    in_maps = []
    for c in range(_N_CORES):
        m = dict(shared)
        m["x"] = np.ascontiguousarray(x[_NB * c:_NB * (c + 1)])
        m["wgsh"] = np.ascontiguousarray(wg[352 * c:352 * (c + 1)])
        m["wush"] = np.ascontiguousarray(wu[352 * c:352 * (c + 1)])
        m["wdsh"] = np.ascontiguousarray(wd[:, 352 * c:352 * (c + 1)])
        in_maps.append(m)
    res = run_bass_kernel_spmd(nc, in_maps, core_ids=list(range(_N_CORES)))
    out = np.concatenate([np.asarray(res.results[c]["out"]) for c in range(_N_CORES)], axis=0)
    return out.astype(np.float32, copy=False)


# version 23 (final)
# speedup vs baseline: 1.1635x; 1.0242x over previous
import math
import contextlib
import numpy as np
import ml_dtypes
import concourse.bass as bass
import concourse.mybir as mybir

F32 = mybir.dt.float32
BF16 = mybir.dt.bfloat16
I32 = mybir.dt.int32
I16 = mybir.dt.int16
AF = mybir.ActivationFunctionType
ALU = mybir.AluOpType
AX = mybir.AxisListType

L, DM, H, DH, R, IF = 2048, 1024, 16, 64, 512, 2816
NTC, NKC, NIC = L // 128, DM // 128, IF // 128
GRPS = [(g * 512, 512) for g in range(5)] + [(2560, 256)]
EPS = 1e-5
QSCALE = 1.0 / math.sqrt(R)
ENGINES = ("sync", "tensor", "vector", "scalar", "gpsimd")
QNAMES = ("u", "p1", "c1", "c2", "B", "s1", "s2", "s3")


def host_constants():
    c = {}
    c["ident_f32"] = np.eye(128, dtype=np.float32)
    c["ident_bf16"] = np.eye(128, dtype=ml_dtypes.bfloat16)
    c["one_f32"] = np.ones((1, 1), dtype=np.float32)
    c["onescol_f32"] = np.ones((1, 128), dtype=np.float32)
    crows = np.zeros((128, 2), dtype=np.float32)
    crows[:, 0] = 4.0
    c["crows"] = crows

    def fbasis(t, m):
        s = math.sqrt((1.0 if m == 0 else 2.0) / L)
        return np.cos(np.pi * (np.abs(t) + 0.5) * m / L) * s

    rr = np.arange(128)[:, None]
    cc = np.arange(256)[None, :]
    t = cc - rr - 64
    band = (t >= -31) & (t <= 32)
    Cm = np.stack([fbasis(t, m) * band for m in range(6)])
    c["slab_basis"] = np.ascontiguousarray(Cm.transpose(1, 0, 2).reshape(128, 6 * 256)).astype(np.float32)
    kk = np.arange(64)[:, None]
    qq = np.arange(32)[None, :]
    t0 = kk - qq
    C0 = np.stack([fbasis(t0, m) * (t0 >= 32) for m in range(6)])
    kL = (1984 + np.arange(64))[:, None]
    qL = (2016 + np.arange(32))[None, :]
    tL = qL - kL
    mL = (tL >= 33) & (kL <= 2014)
    CL = np.stack([fbasis(tL, m) * mL for m in range(6)])
    c["corr0_basis"] = np.ascontiguousarray(C0.transpose(1, 0, 2).reshape(64, 6 * 32)).astype(np.float32)
    c["corrL_basis"] = np.ascontiguousarray(CL.transpose(1, 0, 2).reshape(64, 6 * 32)).astype(np.float32)
    return c


class SemW:
    __slots__ = ("h", "_v")
    def __init__(self, h):
        self.h, self._v = h, 0


class Emitter:
    def __init__(self, nc, es):
        self.nc, self.es = nc, es
        self.prog = {e: [] for e in ENGINES}

    def sem(self, name):
        return SemW(self.es.enter_context(self.nc.semaphore(name)))

    def emit(self, eng, fn, waits=(), inc=None, incn=1):
        waits = [(s, (v if v is not None else s._v)) for (s, v) in waits]
        if inc is not None:
            inc._v += incn
        def thunk(e, fn=fn, waits=waits, inc=inc, incn=incn):
            for s, v in waits:
                if v > 0:
                    e.wait_ge(s.h, v)
            ins = fn(e)
            if inc is not None:
                assert ins is not None, "inc requires fn to return its last instruction"
                ins.then_inc(inc.h, incn)
        self.prog[eng].append(thunk)
        return inc._v if inc is not None else None

    def wait_only(self, eng, waits):
        waits = [(s, (v if v is not None else s._v)) for (s, v) in waits]
        def thunk(e, waits=waits):
            for s, v in waits:
                if v > 0:
                    e.wait_ge(s.h, v)
        self.prog[eng].append(thunk)

    def fenced(self, eng, fn, sem, waits=()):
        v = self.emit(eng, fn, waits=waits, inc=sem)
        self.wait_only(eng, [(sem, v)])
        return v

    def run(self, block):
        block.sync(lambda e: [f(e) for f in self.prog["sync"]])
        block.tensor(lambda e: [f(e) for f in self.prog["tensor"]])
        block.vector(lambda e: [f(e) for f in self.prog["vector"]])
        block.scalar(lambda e: [f(e) for f in self.prog["scalar"]])
        block.gpsimd(lambda e: [f(e) for f in self.prog["gpsimd"]])


def build_core_program(NB=2, debug_dump=()):
    nc = bass.Bass("TRN2", target_bir_lowering=False)
    es = contextlib.ExitStack()
    nc._keepalive_es = es

    def din(name, shape, dtype=F32):
        return nc.dram_tensor(name, shape, dtype, kind="ExternalInput").ap()

    x_d = din("x", [NB, L, DM])
    wq_d, bq_d = din("wq", [DM, DM]), din("bq", [DM])
    wk_d, bk_d = din("wk", [DM, DM]), din("bk", [DM])
    wv_d, bv_d = din("wv", [DM, DM]), din("bv", [DM])
    coef_d = din("coef", [H, 6])
    qproj_d, kproj_d = din("qproj", [H, DH, R]), din("kproj", [H, DH, R])
    wgsh_d = din("wgsh", [IF // 8, DM])
    wush_d = din("wush", [IF // 8, DM])
    wdsh_d = din("wdsh", [DM, IF // 8])
    identf_d = din("ident_f32", [128, 128])
    identb_d = din("ident_bf16", [128, 128], BF16)
    onef_d = din("one_f32", [1, 1])
    onescolf_d = din("onescol_f32", [1, 128])
    crows_d = din("crows", [128, 2])
    slabB_d = din("slab_basis", [128, 6 * 256])
    corr0B_d = din("corr0_basis", [64, 6 * 32])
    corrLB_d = din("corrL_basis", [64, 6 * 32])

    out_d = nc.dram_tensor("out", [NB, L, DM], F32, kind="ExternalOutput").ap()

    ctx_dram = nc.dram_tensor("ctx_scr", [L, DM], BF16, kind="Internal").ap()
    SH = IF // 8
    wQl = {}
    wQ_gout = {}
    wQ_ld = {}
    for tag in ("g", "u"):
        wQl[tag] = nc.dram_tensor("wQl_%s" % tag, [DM, SH], BF16, kind="Internal").ap()
        flat = nc.dram_tensor("wQg_%s" % tag, [8 * DM * SH], BF16, kind="Internal").ap()
        wQ_gout[tag] = flat.rearrange("(a s) -> a s", s=SH)
        wQ_ld[tag] = flat.rearrange("(r mc p s) -> mc p r s", r=8, mc=NKC, p=128)
    wQl["d"] = nc.dram_tensor("wQl_d", [SH, DM], BF16, kind="Internal").ap()
    flat_d = nc.dram_tensor("wQg_d", [IF * DM], BF16, kind="Internal").ap()
    wQ_gout["d"] = flat_d.rearrange("(a m) -> a m", m=DM)
    wQ_ld["d"] = flat_d.rearrange("(a m) -> a m", m=DM)

    dbg = {}
    def ddump(name, shape, dtype):
        if name in debug_dump:
            dbg[name] = nc.dram_tensor("dbg_" + name, list(shape), dtype, kind="ExternalOutput").ap()
    ddump("xln", (128, NKC * L), BF16)
    ddump("qT", (128, NKC * L), BF16)
    ddump("kT", (128, NKC * L), BF16)
    ddump("vaug", (128, NTC * 1040), BF16)
    ddump("km0", (128, 2 * L), BF16)
    ddump("pt00", (128, 16 * 512), BF16)
    ddump("slab0", (128, 256), BF16)
    ddump("ctx", (128, NTC * DM), BF16)
    ddump("wgq", (128, NIC * DM), BF16)
    ddump("x1t", (128, DM), F32)
    ddump("hq", (128, IF), BF16)

    E = Emitter(nc, es)
    sb = lambda name, shape, dtype: es.enter_context(nc.sbuf_tensor(name, shape, dtype))

    identb = sb("identb", [128, 128], BF16)
    identf = sb("identf", [128, 128], F32)
    onef = sb("onef", [1, 1], F32)
    onescolb = sb("onescolb", [1, 128], BF16)
    crows = sb("crows_sb", [128, 2], F32)
    slabs = sb("slabs", [128, H * 256], BF16)
    corr0s = sb("corr0s", [64, H * 32], BF16)
    corrLs = sb("corrLs", [64, H * 32], BF16)
    bqT = sb("bqT", [128, NKC], F32)
    bkT = sb("bkT", [128, NKC], F32)
    Nsb = sb("Nsb", [128, (H // 2) * 64], BF16)
    lnst = sb("lnst", [128, 12], F32)
    sqjunk = sb("sqjunk", [128, DM], BF16)
    mrow = sb("mrow", [128, NTC], F32)
    rrow = sb("rrow", [128, NTC], F32)
    nbias = sb("nbias", [128, NTC], F32)

    s_ld = E.sem("s_ld")
    s_cc = E.sem("s_cc")
    s_ldx = E.sem("s_ldx")
    s_mm = E.sem("s_mm")
    s_ve = E.sem("s_ve")
    s_se = E.sem("s_se")
    s_ge = E.sem("s_ge")
    s_st = E.sem("s_st")

    QTMP = {}

    def quant_head(src_ap, Wd, T, mi, waits_v=(), P=128):
        nb = Wd // 16
        v3 = lambda ap: ap.rearrange("p (b k) -> p b k", k=16)
        mT = T["m"]
        bmv = mT[0:P, 192 * mi:192 * mi + nb]
        rsv = mT[0:P, 192 * mi + 64:192 * mi + 64 + nb]
        shv = mT[0:P, 192 * mi + 128:192 * mi + 128 + nb]
        v1 = E.emit("vector", lambda e: e.tensor_reduce(
            bmv, v3(src_ap), AX.X, ALU.max, apply_absolute_value=True),
            waits=list(waits_v) + [(s_ge, T.get("mge", 0)), (s_se, T.get("mse", 0))], inc=s_ve)
        yield
        E.wait_only("vector", [(s_ve, v1)])
        v2 = E.emit("vector", lambda e: e.tensor_scalar(rsv, bmv, 1.0 / 6.0, 1e-12, ALU.mult, ALU.max), inc=s_ve)
        yield
        E.wait_only("vector", [(s_ve, v2)])
        E.emit("vector", lambda e: e.tensor_scalar(shv, rsv, 0.5, None, ALU.mult))
        v3b = E.emit("vector", lambda e: e.reciprocal(bmv, rsv), inc=s_ve)
        yield
        E.wait_only("vector", [(s_ve, v3b)])

    def quant_tail(src_ap, out_ap, Wd, T, mi, waits_q=(), P=128):
        nb = Wd // 16
        v3 = lambda ap: ap.rearrange("p (b k) -> p b k", k=16)
        bc = lambda ap: ap.unsqueeze(2).broadcast_to((P, nb, 16))
        fT, iT, mT = T["f"], T["i"], T["m"]
        sl = lambda j: (T["c2"][0:P, 0:Wd] if j == 3 else fT[0:P, 512 * j:512 * j + Wd])
        il = lambda j: iT[0:P, 512 * j:512 * j + Wd]
        bmv = mT[0:P, 192 * mi:192 * mi + nb]
        shv = mT[0:P, 192 * mi + 128:192 * mi + 128 + nb]
        uv = E.emit("vector", lambda e: e.tensor_tensor(v3(sl(0)), v3(src_ap), bc(bmv), ALU.mult), inc=s_ve)
        sv = E.emit("scalar", lambda e: e.activation(sl(1), sl(0), AF.Relu,
                                                     bias=crows[0:P, 0:1], scale=2.0),
                    waits=[(s_ve, uv), (s_ge, T.get("mge", 0))], inc=s_se)
        T["mse"] = sv
        E.emit("vector", lambda e: e.tensor_scalar(il(1), sl(0), -4.0, 4.0, ALU.max, ALU.min))
        E.emit("vector", lambda e: e.tensor_scalar(il(2), sl(0), -2.0, 2.0, ALU.max, ALU.min))
        E.emit("vector", lambda e: e.tensor_scalar(sl(3), sl(0), -5.0, -4.0, ALU.is_lt, ALU.mult))
        cv = E.emit("vector", lambda e: e.tensor_scalar(sl(2), sl(0), 5.0, 4.0, ALU.is_gt, ALU.mult), inc=s_ve)
        vA = E.emit("vector", lambda e: e.tensor_scalar(il(0), sl(1), 8.0, None, ALU.min),
                    waits=[(s_se, sv)], inc=s_ve)
        gv = E.emit("gpsimd", lambda e: e.tensor_tensor(sl(1), il(1), il(2), ALU.subtract),
                    waits=[(s_ve, vA)], inc=s_ge)
        gv2 = E.emit("gpsimd", lambda e: e.tensor_tensor(sl(3), sl(2), sl(3), ALU.add),
                     waits=[(s_ve, cv)], inc=s_ge)
        T["mge"] = gv2
        E.emit("vector", lambda e: e.scalar_tensor_tensor(sl(0), sl(1), 2.0, il(0), ALU.mult, ALU.add),
               waits=[(s_ge, gv)])
        E.emit("vector", lambda e: e.scalar_tensor_tensor(sl(2), sl(0), -4.0, sl(3), ALU.add, ALU.add),
               waits=[(s_ge, gv2)])
        return E.emit("vector", lambda e: e.tensor_tensor(v3(out_ap), v3(sl(2)), bc(shv), ALU.mult),
                      waits=list(waits_q), inc=s_ve)

    def quant_pair(a, bq):
        P_ = lambda x: x[7] if len(x) > 7 else 128
        gens = [quant_head(x[0], x[2], x[3], x[4], waits_v=x[5], P=P_(x)) for x in (a, bq) if x]
        alive = list(gens)
        while alive:
            for g in list(alive):
                try:
                    next(g)
                except StopIteration:
                    alive.remove(g)
        last = None
        for x in (a, bq):
            if x:
                last = quant_tail(x[0], x[1], x[2], x[3], x[4], waits_q=x[6], P=P_(x))
        return last

    def emit_quant(src_ap, out_ap, Wd, waits_v=(), waits_q=(), T=None):
        if T is None:
            T = QTMP
        return quant_pair((src_ap, out_ap, Wd, T, 0, list(waits_v), list(waits_q)), None)

    def emit_ln(x_ap, m_dst, r_dst, nb_dst, waits0=()):
        E.fenced("vector", lambda e: e.tensor_reduce(lnst[:, 0:1], x_ap, AX.X, ALU.add),
                 s_ve, waits=list(waits0))
        mv = E.emit("vector", lambda e: e.tensor_scalar(m_dst, lnst[:, 0:1], 1.0 / DM, None, ALU.mult), inc=s_ve)
        sqv = E.emit("scalar", lambda e: e.activation(sqjunk[:], x_ap, AF.Square, bias=m_dst,
                                                      scale=-1.0, accum_out=lnst[:, 1:2]),
                     waits=[(s_ve, mv)], inc=s_se)
        vv = E.emit("vector", lambda e: e.tensor_scalar(lnst[:, 2:3], lnst[:, 1:2], 1.0 / DM, EPS, ALU.mult, ALU.add),
                    waits=[(s_se, sqv)], inc=s_ve)
        sq2 = E.emit("scalar", lambda e: e.activation(lnst[:, 3:4], lnst[:, 2:3], AF.Sqrt),
                     waits=[(s_ve, vv)], inc=s_se)
        E.fenced("vector", lambda e: e.reciprocal(lnst[:, 4:5], lnst[:, 3:4]), s_ve, waits=[(s_se, sq2)])
        E.fenced("vector", lambda e: e.tensor_tensor(lnst[:, 5:6], lnst[:, 4:5], lnst[:, 4:5], ALU.mult), s_ve)
        E.fenced("vector", lambda e: e.tensor_tensor(lnst[:, 6:7], lnst[:, 5:6], lnst[:, 2:3], ALU.mult), s_ve)
        E.fenced("vector", lambda e: e.tensor_scalar(lnst[:, 6:7], lnst[:, 6:7], -0.5, 1.5, ALU.mult, ALU.add), s_ve)
        E.fenced("vector", lambda e: e.tensor_tensor(r_dst, lnst[:, 4:5], lnst[:, 6:7], ALU.mult), s_ve)
        E.fenced("vector", lambda e: e.tensor_tensor(lnst[:, 7:8], m_dst, r_dst, ALU.mult), s_ve)
        return E.emit("vector", lambda e: e.tensor_scalar(nb_dst, lnst[:, 7:8], -1.0, None, ALU.mult), inc=s_ve)

    with contextlib.ExitStack() as _es0:
        slabB = _es0.enter_context(nc.sbuf_tensor("slabB", [128, 6 * 256], F32))
        corr0B = _es0.enter_context(nc.sbuf_tensor("corr0B", [64, 6 * 32], F32))
        corrLB = _es0.enter_context(nc.sbuf_tensor("corrLB", [64, 6 * 32], F32))
        brow = _es0.enter_context(nc.sbuf_tensor("brow", [1, 3 * DM], F32))
        coefrow = _es0.enter_context(nc.sbuf_tensor("coefrow", [1, 6 * H], F32))
        coefb = _es0.enter_context(nc.sbuf_tensor("coefb", [128, 6], F32))
        qscale_t = _es0.enter_context(nc.sbuf_tensor("qscale_t", [1, 1], F32))

        onescolf = _es0.enter_context(nc.sbuf_tensor("onescolf", [1, 128], F32))
        slab_acc = _es0.enter_context(nc.sbuf_tensor("slab_acc", [128, 2 * 256], F32))
        wchunk = _es0.enter_context(nc.sbuf_tensor("wchunk", [128, 2 * 3072], F32))
        wqn = _es0.enter_context(nc.sbuf_tensor("wqn", [128, 2 * 3072], BF16))
        wqnT = _es0.enter_context(nc.sbuf_tensor("wqnT", [128, 2 * 3072], BF16))
        ps_b0 = _es0.enter_context(nc.psum_tensor("ps_b0", [128, 16], F32))
        ps_cf = _es0.enter_context(nc.psum_tensor("ps_cf", [128, 6], F32))
        ps_w = (_es0.enter_context(nc.psum_tensor("ps_w0", [128, 128], BF16)),
                _es0.enter_context(nc.psum_tensor("ps_w1", [128, 128], BF16)))
        for dst, src in ((identf, identf_d), (identb, identb_d), (onef, onef_d),
                         (onescolf, onescolf_d), (crows, crows_d), (slabB, slabB_d),
                         (corr0B, corr0B_d), (corrLB, corrLB_d)):
            E.emit("sync", lambda e, d=dst, s=src: e.dma_start(out=d[:], in_=s[:]), inc=s_ld, incn=16)
        E.emit("sync", lambda e: e.dma_start(out=brow[0:1, 0:DM], in_=bq_d.unsqueeze(0)), inc=s_ld, incn=16)
        E.emit("sync", lambda e: e.dma_start(out=brow[0:1, DM:2 * DM], in_=bk_d.unsqueeze(0)), inc=s_ld, incn=16)
        E.emit("sync", lambda e: e.dma_start(out=brow[0:1, 2 * DM:3 * DM], in_=bv_d.unsqueeze(0)), inc=s_ld, incn=16)
        for h in range(H):
            E.emit("sync", lambda e, h=h: e.dma_start(out=coefrow[0:1, 6 * h:6 * h + 6], in_=coef_d[h:h + 1, :]), inc=s_ld, incn=16)
        HCR = (H // 2) * R
        ld_consts = s_ld._v
        E.emit("vector", lambda e: e.tensor_copy(onescolb[:], onescolf[:]), waits=[(s_ld, ld_consts)], inc=s_ve)
        E.emit("vector", lambda e: e.memset(qscale_t[:], QSCALE), inc=s_ve)
        ve_consts = s_ve._v

        for i in range(NKC):
            E.emit("tensor", lambda e, i=i: e.matmul(
                ps_b0[:, i:i + 1], brow[0:1, 128 * i:128 * (i + 1)], qscale_t[:],
                start=True, stop=True), waits=[(s_ld, ld_consts), (s_ve, ve_consts)])
        mm_b = 0
        for i in range(NKC):
            mm_b = E.emit("tensor", lambda e, i=i: e.matmul(
                ps_b0[:, 8 + i:9 + i], brow[0:1, DM + 128 * i:DM + 128 * (i + 1)], onef[:],
                start=True, stop=True), inc=(s_mm if i == NKC - 1 else None))
        E.emit("vector", lambda e: e.tensor_copy(bqT[:], ps_b0[:, 0:8]), waits=[(s_mm, mm_b)])
        ve_bias = E.emit("vector", lambda e: e.tensor_copy(bkT[:], ps_b0[:, 8:16]), inc=s_ve)

        for h in range(H):
            mmv = E.emit("tensor", lambda e, h=h: e.matmul(
                ps_cf[:], onescolf[:], coefrow[0:1, 6 * h:6 * h + 6],
                start=True, stop=True), waits=[(s_ve, None)], inc=s_mm)
            E.fenced("vector", lambda e: e.tensor_copy(coefb[:], ps_cf[:]), s_ve, waits=[(s_mm, mmv)])
            E.emit("vector", lambda e: e.tensor_scalar(
                slab_acc[:, 0:256], slabB[:, 0:256], coefb[:, 0:1], None, ALU.mult))
            for m in range(1, 6):
                last = m == 5
                E.emit("vector", lambda e, m=m, h=h, last=last: e.scalar_tensor_tensor(
                    (slabs[:, 256 * h:256 * (h + 1)] if last else slab_acc[:, 256 * (m % 2):256 * (m % 2) + 256]),
                    slabB[:, 256 * m:256 * (m + 1)], coefb[:, m:m + 1],
                    slab_acc[:, 256 * ((m - 1) % 2):256 * ((m - 1) % 2) + 256],
                    ALU.mult, ALU.add), inc=(s_ve if last else None))
            for Bt, St in ((corr0B, corr0s), (corrLB, corrLs)):
                E.fenced("vector", lambda e, Bt=Bt: e.tensor_scalar(
                    slab_acc[0:64, 0:32], Bt[:, 0:32], coefb[0:64, 0:1], None, ALU.mult), s_ve)
                for m in range(1, 6):
                    last = m == 5
                    E.fenced("vector", lambda e, m=m, h=h, last=last, Bt=Bt, St=St: e.scalar_tensor_tensor(
                        (St[:, 32 * h:32 * (h + 1)] if last else slab_acc[0:64, 32 * (m % 2):32 * (m % 2) + 32]),
                        Bt[:, 32 * m:32 * (m + 1)], coefb[0:64, m:m + 1],
                        slab_acc[0:64, 32 * ((m - 1) % 2):32 * ((m - 1) % 2) + 32],
                        ALU.mult, ALU.add), s_ve)
        ve_slabs = s_ve._v
        if "slab0" in dbg:
            E.emit("sync", lambda e: e.dma_start(out=dbg["slab0"], in_=slabs[:, 0:256]),
                   waits=[(s_ve, ve_slabs)], inc=s_st, incn=16)

        projW0 = _es0.enter_context(nc.sbuf_tensor("projW0", [128, 2 * HCR], BF16))
        pqT0 = _es0.enter_context(nc.sbuf_tensor("pqT0", [128, 256], BF16))
        pkT0 = _es0.enter_context(nc.sbuf_tensor("pkT0", [128, 256], BF16))
        ps_pq = (_es0.enter_context(nc.psum_tensor("ps_pq0", [128, 64], BF16)),
                 _es0.enter_context(nc.psum_tensor("ps_pq1", [128, 64], BF16)))
        ps_N = _es0.enter_context(nc.psum_tensor("ps_N", [128, 64], F32))
        for par_ in (0, 1):
            E.emit("gpsimd", lambda e, par_=par_: e.dma_start(
                out=projW0[64 * par_:64 * par_ + 64, 0:HCR].rearrange("d (c r) -> d c r", c=H // 2),
                in_=qproj_d.rearrange("(c par) d r -> par d c r", par=2)[par_]),
                inc=s_ld, incn=16)
            E.emit("gpsimd", lambda e, par_=par_: e.dma_start(
                out=projW0[64 * par_:64 * par_ + 64, HCR:2 * HCR].rearrange("d (c r) -> d c r", c=H // 2),
                in_=kproj_d.rearrange("(c par) d r -> par d c r", par=2)[par_]),
                inc=s_ld, incn=16)
        ld_pj = s_ld._v
        se_pqe = [0, 0]
        mmN_last = 0
        ve_N = 0
        for h in range(H):
            hrow = 64 * (h % 2)
            hcol = R * (h // 2)
            for rc in range(4):
                for pp, (src0, dstT) in enumerate(((hcol, pqT0), (HCR + hcol, pkT0))):
                    mt = E.emit("tensor", lambda e, hrow=hrow, src0=src0, rc=rc, pp=pp: e.transpose(
                        ps_pq[pp][:, 0:64],
                        projW0[hrow:hrow + 64, src0 + 128 * rc: src0 + 128 * (rc + 1)],
                        identb[hrow:hrow + 64, hrow:hrow + 64]),
                        waits=[(s_ld, ld_pj), (s_ld, ld_consts), (s_se, se_pqe[pp])], inc=s_mm)
                    se_pqe[pp] = E.emit("scalar", lambda e, rc=rc, pp=pp, dstT=dstT: e.activation(
                        dstT[:, 64 * rc:64 * (rc + 1)], ps_pq[pp][:, 0:64], AF.Copy),
                        waits=[(s_mm, mt), (s_mm, mmN_last)], inc=s_se)
            evh = s_se._v
            for rc in range(4):
                mmN_last = E.emit("tensor", lambda e, hrow=hrow, rc=rc: e.matmul(
                    ps_N[hrow:hrow + 64, 0:64], pkT0[:, 64 * rc:64 * (rc + 1)],
                    pqT0[:, 64 * rc:64 * (rc + 1)],
                    start=(rc == 0), stop=(rc == 3)),
                    waits=([(s_se, evh), (s_ve, ve_N)] if rc == 0 else []), inc=s_mm)
            ve_N = E.emit("vector", lambda e, hrow=hrow, h=h: e.tensor_copy(
                Nsb[hrow:hrow + 64, 64 * (h // 2):64 * (h // 2) + 64],
                ps_N[hrow:hrow + 64, 0:64]),
                waits=[(s_mm, mmN_last)], inc=s_ve)

        QT2 = []
        for si in range(2):
            QT2.append({
                "f": _es0.enter_context(nc.sbuf_tensor("q0f%d" % si, [128, 3 * 512], F32)),
                "i": _es0.enter_context(nc.sbuf_tensor("q0i%d" % si, [128, 3 * 512], I16)),
                "c2": _es0.enter_context(nc.sbuf_tensor("q0c2%d" % si, [128, 512], F32)),
                "m": _es0.enter_context(nc.sbuf_tensor("q0m%d" % si, [128, 384], F32)),
            })
        STILES = ((0, 0, 128), (1, 128, 128), (2, 256, 96))
        st_wql = {0: 0, 1: 0}
        ve_wq = {0: ve_consts, 1: ve_consts}
        mm_wqT = {0: 0, 1: 0}
        se_evac = [0, 0]
        for ti, (tag, src_d) in enumerate((("g", wgsh_d), ("u", wush_d), ("d", wdsh_d))):
            par = ti % 2
            base = 3072 * par
            if tag != "d":
                lw = []
                for si, s0, sw in STILES:
                    lw.append(E.emit("sync", lambda e, s=src_d, si=si, s0=s0, sw=sw, base=base: e.dma_start(
                        out=wchunk[0:sw, base + 1024 * si: base + 1024 * si + 1024],
                        in_=s[s0:s0 + sw, :]),
                        waits=[(s_ve, ve_wq[par])], inc=s_ld, incn=16))
                for si, s0, sw in STILES:
                    pr = []
                    for k2, g0 in enumerate((0, 512)):
                        pr.append((wchunk[0:sw, base + 1024 * si + g0: base + 1024 * si + g0 + 512],
                                   wqn[0:sw, base + 1024 * si + g0: base + 1024 * si + g0 + 512], 512,
                                   QT2[k2], 0, [(s_ld, lw[si])], [(s_mm, mm_wqT[par])], sw))
                    ve_wq[par] = quant_pair(pr[0], pr[1])
                ve_q = ve_wq[par]
                for si, s0, sw in STILES:
                    for mc in range(NKC):
                        pp = (si * NKC + mc) % 2
                        mt = E.emit("tensor", lambda e, si=si, sw=sw, mc=mc, pp=pp, base=base: e.transpose(
                            ps_w[pp][:, 0:sw],
                            wqn[0:sw, base + 1024 * si + 128 * mc: base + 1024 * si + 128 * (mc + 1)],
                            identb[0:sw, 0:sw]),
                            waits=[(s_ve, ve_q), (s_se, se_evac[pp]), (s_ld, ld_consts)], inc=s_mm)
                        se_evac[pp] = E.emit("scalar", lambda e, s0=s0, sw=sw, mc=mc, pp=pp, base=base: e.activation(
                            wqnT[:, base + SH * mc + s0: base + SH * mc + s0 + sw],
                            ps_w[pp][:, 0:sw], AF.Copy),
                            waits=[(s_mm, mt), (s_st, st_wql[par])], inc=s_se)
                mm_wqT[par] = s_mm._v
                se_done = s_se._v
                st_wql[par] = E.emit("sync", lambda e, tag=tag, base=base: e.dma_start(
                    out=wQl[tag].rearrange("(mc p) s -> p mc s", p=128),
                    in_=wqnT[:, base: base + 2816].rearrange("p (mc s) -> p mc s", mc=NKC)),
                    waits=[(s_se, se_done)], inc=s_st, incn=16)
            else:
                lwd0 = E.emit("sync", lambda e, s=src_d, base=base: e.dma_start(
                    out=wchunk[:, base: base + 2816].rearrange("p (mc s) -> p mc s", mc=NKC),
                    in_=s.rearrange("(mc p) s -> p mc s", p=128)),
                    waits=[(s_ve, ve_wq[par])], inc=s_ld, incn=16)
                for gi in range(0, NKC, 2):
                    pr = []
                    for k2 in range(2):
                        g0 = SH * (gi + k2)
                        pr.append((wchunk[:, base + g0: base + g0 + SH],
                                   wqn[:, base + g0: base + g0 + SH], SH,
                                   QT2[k2], 0, [(s_ld, lwd0)], [(s_mm, mm_wqT[par])], 128))
                    ve_wq[par] = quant_pair(pr[0], pr[1])
                ve_q = ve_wq[par]
                for mc in range(NKC):
                    for si, s0, sw in STILES:
                        pp = (mc * 3 + si) % 2
                        mt = E.emit("tensor", lambda e, mc=mc, s0=s0, sw=sw, pp=pp, base=base: e.transpose(
                            ps_w[pp][0:sw, :],
                            wqn[:, base + SH * mc + s0: base + SH * mc + s0 + sw],
                            identb[:, :]),
                            waits=[(s_ve, ve_q), (s_se, se_evac[pp]), (s_ld, ld_consts)], inc=s_mm)
                        se_evac[pp] = E.emit("scalar", lambda e, mc=mc, si=si, sw=sw, pp=pp, base=base: e.activation(
                            wqnT[0:sw, base + 1024 * si + 128 * mc: base + 1024 * si + 128 * (mc + 1)],
                            ps_w[pp][0:sw, :], AF.Copy),
                            waits=[(s_mm, mt), (s_st, st_wql[par])], inc=s_se)
                mm_wqT[par] = s_mm._v
                se_done = s_se._v
                for si, s0, sw in STILES:
                    st_wql[par] = E.emit("sync", lambda e, si=si, s0=s0, sw=sw, base=base: e.dma_start(
                        out=wQl["d"][s0:s0 + sw, :],
                        in_=wqnT[0:sw, base + 1024 * si: base + 1024 * si + 1024]),
                        waits=[(s_se, se_done)], inc=s_st, incn=16)
            E.emit("gpsimd", lambda e, tag=tag: e.collective_compute(
                "AllGather", mybir.AluOpType.bypass,
                replica_groups=[list(range(8))],
                ins=[wQl[tag].opt()], outs=[wQ_gout[tag].opt()]),
                waits=[(s_st, st_wql[par])], inc=s_cc, incn=1)
        cc_wq_done = s_cc._v

    ve_xt_free_start = 0
    st_ctx_rd = 0
    st_ctx_prev = 0
    st_out_last = 0

    for b in range(NB):
        st_bgate = s_st._v
        with contextlib.ExitStack() as _esb:
            qT = _esb.enter_context(nc.sbuf_tensor("qT%d" % b, [128, NKC * L], BF16))
            kT = _esb.enter_context(nc.sbuf_tensor("kT%d" % b, [128, NKC * L], BF16))
            vaug = _esb.enter_context(nc.sbuf_tensor("vaug%d" % b, [128, NTC * 1040], BF16))
            xlnT_ready = [0] * NTC
            with contextlib.ExitStack() as _esa:
                xlnT = _esa.enter_context(nc.sbuf_tensor("xlnT%d" % b, [128, NKC * L], BF16))
                wqkv = _esa.enter_context(nc.sbuf_tensor("wqkv%d" % b, [128, NKC * DM], BF16))
                xt2 = _esa.enter_context(nc.sbuf_tensor("xt2%d" % b, [128, 2 * DM], F32))
                xlntile = _esa.enter_context(nc.sbuf_tensor("xlntile%d" % b, [128, DM], BF16))

                bvrow = _esa.enter_context(nc.sbuf_tensor("bvrow%d" % b, [1, DM], BF16))
                ps_a0 = _esa.enter_context(nc.psum_tensor("ps_a0%d" % b, [128, 512], F32))
                ps_a1 = _esa.enter_context(nc.psum_tensor("ps_a1%d" % b, [128, 512], F32))
                ps_t0 = _esa.enter_context(nc.psum_tensor("ps_t0%d" % b, [128, 128], BF16))
                ps_t1 = _esa.enter_context(nc.psum_tensor("ps_t1%d" % b, [128, 128], BF16))
                ps_a = (ps_a0, ps_a1)
                ps_t = (ps_t0, ps_t1)
                ev_tp = [0, 0]
                xt_free = [ve_xt_free_start] * NTC
                se_xln_prev = 0
                for tc in range(NTC):
                    par = tc % 2
                    ldx = E.emit("sync", lambda e, b=b, tc=tc, par=par: e.dma_start(
                        out=xt2[:, DM * par:DM * (par + 1)], in_=x_d[b, 128 * tc:128 * (tc + 1), :]),
                        waits=[(s_ve, xt_free[tc - 2] if tc >= 2 else ve_xt_free_start),
                               (s_se, se_xln_prev), (s_st, st_bgate)], inc=s_ldx, incn=16)
                    xt = xt2[:, DM * par:DM * (par + 1)]
                    nbv = emit_ln(xt, lnst[:, 8:9], lnst[:, 9:10], lnst[:, 10:11],
                                  waits0=[(s_ldx, ldx)])
                    xt_free[tc] = nbv
                    lv = E.emit("scalar", lambda e, xt=xt: e.activation(
                        xlntile[:], xt, AF.Identity, bias=lnst[:, 10:11], scale=lnst[:, 9:10]),
                        waits=[(s_ve, nbv)], inc=s_se)
                    se_xln_prev = lv
                    for c in range(NKC):
                        tpar = c % 2
                        mt = E.emit("tensor", lambda e, c=c, tpar=tpar: e.transpose(
                            ps_t[tpar][:], xlntile[:, 128 * c:128 * (c + 1)], identb[:]),
                            waits=[(s_se, lv), (s_ve, ev_tp[tpar])], inc=s_mm)
                        ev_tp[tpar] = E.emit("vector", lambda e, c=c, tc=tc, tpar=tpar: e.tensor_copy(
                            xlnT[:, L * c + 128 * tc: L * c + 128 * (tc + 1)], ps_t[tpar][:]),
                            waits=[(s_mm, mt)], inc=s_ve)
                    xlnT_ready[tc] = ev_tp[(NKC - 1) % 2]

                if "xln" in dbg:
                    E.emit("sync", lambda e: e.dma_start(out=dbg["xln"], in_=xlnT[:]),
                           waits=[(s_ve, xlnT_ready[NTC - 1])], inc=s_st, incn=16)

                ev_ps_a = [0, 0]
                for (wsrc, bT, scl, dstT, dbgk) in ((wq_d, bqT, QSCALE, qT, "qT"), (wk_d, bkT, 1.0, kT, "kT")):
                    ldw = [0] * NKC
                    for kc in range(NKC):
                        ldw[kc] = E.emit("gpsimd", lambda e, wsrc=wsrc, kc=kc: e.dma_start(
                            out=wqkv[:, DM * kc:DM * (kc + 1)], in_=wsrc[128 * kc:128 * (kc + 1), :]),
                            waits=([(s_mm, None), (s_st, st_bgate)] if kc == 0 else []) + [(s_se, ev_ps_a[0]), (s_se, ev_ps_a[1])],
                            inc=s_ld, incn=16)
                    for nci in range(NKC):
                        for tcol in range(4):
                            par = (nci * 4 + tcol) % 2
                            def mmq(e, nci=nci, tcol=tcol, par=par):
                                ins = None
                                for kc in range(NKC):
                                    ins = e.matmul(ps_a[par][:],
                                                   wqkv[:, DM * kc + 128 * nci:DM * kc + 128 * (nci + 1)],
                                                   xlnT[:, L * kc + 512 * tcol:L * kc + 512 * (tcol + 1)],
                                                   start=(kc == 0), stop=(kc == NKC - 1))
                                return ins
                            mmv = E.emit("tensor", mmq,
                                         waits=[(s_ld, ldw[NKC - 1]), (s_ve, xlnT_ready[4 * tcol + 3]),
                                                (s_se, ev_ps_a[par])], inc=s_mm)
                            ev_ps_a[par] = E.emit("scalar", lambda e, nci=nci, tcol=tcol, par=par, bT=bT, scl=scl, dstT=dstT: e.activation(
                                dstT[:, L * nci + 512 * tcol:L * nci + 512 * (tcol + 1)], ps_a[par][:],
                                AF.Identity, bias=bT[:, nci:nci + 1], scale=scl),
                                waits=[(s_mm, mmv)], inc=s_se)
                    if dbgk in dbg:
                        E.emit("sync", lambda e, dstT=dstT, dbgk=dbgk: e.dma_start(out=dbg[dbgk], in_=dstT[:]),
                               waits=[(s_se, None)], inc=s_st, incn=16)

                ldw = [0] * NKC
                for kc in range(NKC):
                    ldw[kc] = E.emit("gpsimd", lambda e, kc=kc: e.dma_start(
                        out=wqkv[:, DM * kc:DM * (kc + 1)], in_=wv_d[128 * kc:128 * (kc + 1), :]),
                        waits=([(s_mm, None), (s_st, st_bgate)] if kc == 0 else []) + [(s_se, ev_ps_a[0]), (s_se, ev_ps_a[1])],
                        inc=s_ld, incn=16)
                ld_bv = E.emit("gpsimd", lambda e: e.dma_start(
                    out=bvrow[:], in_=bv_d.unsqueeze(0)), waits=[(s_st, st_bgate)], inc=s_ld, incn=16)
                E.emit("vector", lambda e: e.memset(
                    vaug[:].rearrange("p (t h d) -> p t h d", t=NTC, d=65)[:, :, :, 64:65], 1.0),
                    waits=[(s_mm, None)], inc=s_ve)
                ev_ps_v = [0, 0]
                for tc in range(NTC):
                    for nh in range(2):
                        par = (tc * 2 + nh) % 2
                        def mmv_(e, tc=tc, nh=nh, par=par):
                            for kc in range(NKC):
                                e.matmul(ps_a[par][:],
                                         xlnT[:, L * kc + 128 * tc:L * kc + 128 * (tc + 1)],
                                         wqkv[:, DM * kc + 512 * nh:DM * kc + 512 * (nh + 1)],
                                         start=(kc == 0), stop=False)
                            return e.matmul(ps_a[par][:], onescolb[:], bvrow[0:1, 512 * nh:512 * (nh + 1)],
                                            start=False, stop=True)
                        mmv2 = E.emit("tensor", mmv_,
                                      waits=[(s_ld, ldw[NKC - 1]), (s_ve, ev_ps_v[par]),
                                             (s_se, None), (s_ve, None)], inc=s_mm)
                        dst = vaug[:, 1040 * tc + 520 * nh: 1040 * tc + 520 * (nh + 1)]
                        ev_ps_v[par] = E.emit("vector", lambda e, dst=dst, par=par: e.tensor_copy(
                            dst.rearrange("p (h d) -> p h d", d=65)[:, :, 0:64],
                            ps_a[par][:].rearrange("p (h d) -> p h d", d=64)),
                            waits=[(s_mm, mmv2)], inc=s_ve)
                ve_vaug = s_ve._v
                if "vaug" in dbg:
                    E.emit("sync", lambda e: e.dma_start(out=dbg["vaug"], in_=vaug[:]),
                           waits=[(s_ve, ve_vaug)], inc=s_st, incn=16)

            with contextlib.ExitStack() as _esk:
                kmT_sb = _esk.enter_context(nc.sbuf_tensor("kmT%d" % b, [128, 2 * L], BF16))
                ptS = _esk.enter_context(nc.sbuf_tensor("ptS%d" % b, [128, 16 * 512], BF16))

                ctxS = _esk.enter_context(nc.sbuf_tensor("ctxS%d" % b, [128, NTC * DM], BF16))

                ctxf = _esk.enter_context(nc.sbuf_tensor("ctxf%d" % b, [65, 512], F32))

                rs_t = _esk.enter_context(nc.sbuf_tensor("rs_t%d" % b, [128, 2], F32))
                ps_kp = _esk.enter_context(nc.psum_tensor("ps_kp%d" % b, [128, 512], F32))
                ps_s0 = _esk.enter_context(nc.psum_tensor("ps_s0%d" % b, [128, 1024], F32))
                ps_s1 = _esk.enter_context(nc.psum_tensor("ps_s1%d" % b, [128, 1024], F32))
                ps_cx = _esk.enter_context(nc.psum_tensor("ps_cx%d" % b, [65, 512], F32))
                ps_ct0 = _esk.enter_context(nc.psum_tensor("ps_ct0%d" % b, [128, 65], F32))
                ps_ct1 = _esk.enter_context(nc.psum_tensor("ps_ct1%d" % b, [128, 65], F32))
                ps_s = (ps_s0, ps_s1)
                ps_ct = (ps_ct0, ps_ct1)
                se_exp = [0, 0]
                ve_cx = [0]
                ve_ct = [0, 0]
                ctx_wr = [0] * NTC
                pending = []
                kp_ev = [(s_ve, 0)]
                kmT_ready = {}

                def queue_kmT(h, first=False):
                    hrow = 64 * (h % 2)
                    c = h // 2
                    cblk = c % 2
                    kmT_ready[h] = []

                    def mk_chunk(chunk):
                        def go():
                            w = [kp_ev[0]]
                            if first and chunk == 0:
                                w += [(s_ve, None), (s_se, None)]
                            mmk = E.emit("tensor", lambda e, hrow=hrow, c=c, chunk=chunk: e.matmul(
                                ps_kp[hrow:hrow + 64, :],
                                Nsb[hrow:hrow + 64, 64 * c:64 * (c + 1)],
                                kT[hrow:hrow + 64, L * c + 512 * chunk: L * c + 512 * (chunk + 1)],
                                start=True, stop=True), waits=w, inc=s_mm)
                            dst = kmT_sb[hrow:hrow + 64, 2048 * cblk + 512 * chunk: 2048 * cblk + 512 * (chunk + 1)]
                            if chunk % 2 == 0:
                                kp_ev[0] = (s_ve, E.emit("vector", lambda e, dst=dst, hrow=hrow: e.tensor_copy(
                                    dst, ps_kp[hrow:hrow + 64, :]), waits=[(s_mm, mmk)], inc=s_ve))
                            else:
                                kp_ev[0] = (s_se, E.emit("scalar", lambda e, dst=dst, hrow=hrow: e.activation(
                                    dst, ps_kp[hrow:hrow + 64, :], AF.Copy), waits=[(s_mm, mmk)], inc=s_se))
                            kmT_ready[h].append(kp_ev[0])
                        return go

                    return [mk_chunk(ch) for ch in range(4)]

                for go in queue_kmT(0, first=True):
                    go()
                for go in queue_kmT(1):
                    go()
                for h in range(H):
                    hrow = 64 * (h % 2)
                    c = h // 2
                    cblk = c % 2
                    for quad in range(4):
                        km_waits = kmT_ready[h] if quad == 0 else []
                        kmq = queue_kmT(h + 2) if (quad == 0 and h + 2 < H) else []
                        q0 = 512 * quad
                        if pending:
                            pending.pop(0)()
                        for p in range(8):
                            par = p % 2
                            mml = 0
                            for j in range(2):
                                kc = 2 * p + j
                                def mmsc(e, kc=kc, j=j, par=par, h=h, quad=quad, q0=q0, hrow=hrow, c=c, cblk=cblk):
                                    bb = 512 * j
                                    lo = max(q0, 128 * kc - 64, 0)
                                    hi = min(q0 + 512, 128 * kc + 192, L)
                                    have_slab = lo < hi
                                    have_c0 = (kc == 0 and quad == 0)
                                    have_cL = (kc == 15 and quad == 3)
                                    n_extra = int(have_slab) + int(have_c0) + int(have_cL)
                                    ins = e.matmul(ps_s[par][:, bb:bb + 512],
                                                   kmT_sb[hrow:hrow + 64, 2048 * cblk + 128 * kc: 2048 * cblk + 128 * (kc + 1)],
                                                   qT[hrow:hrow + 64, L * c + q0: L * c + q0 + 512],
                                                   start=True, stop=(n_extra == 0),
                                                   skip_group_check=True)
                                    k = n_extra
                                    if have_slab:
                                        k -= 1
                                        ins = e.matmul(ps_s[par][:, bb + lo - q0:bb + hi - q0], identb[:],
                                                       slabs[:, 256 * h + lo - (128 * kc - 64): 256 * h + hi - (128 * kc - 64)],
                                                       start=False, stop=(k == 0), skip_group_check=True)
                                    if have_c0:
                                        k -= 1
                                        ins = e.matmul(ps_s[par][0:64, 0:32], identb[0:64, 0:64],
                                                       corr0s[:, 32 * h:32 * (h + 1)],
                                                       start=False, stop=(k == 0), skip_group_check=True)
                                    if have_cL:
                                        k -= 1
                                        ins = e.matmul(ps_s[par][64:128, bb + 480:bb + 512], identb[0:64, 0:64],
                                                       corrLs[:, 32 * h:32 * (h + 1)],
                                                       start=False, stop=(k == 0), skip_group_check=True)
                                    return ins
                                mml = E.emit("tensor", mmsc,
                                             waits=(km_waits if (p == 0 and j == 0) else []) +
                                                   ([(s_se, se_exp[par])] if j == 0 else []), inc=s_mm)
                            se_exp[par] = E.emit("scalar", lambda e, p=p, par=par: e.activation(
                                ptS[:, 1024 * p:1024 * (p + 1)], ps_s[par][:, 0:1024], AF.Exp),
                                waits=[(s_mm, mml)], inc=s_se)
                            if p >= 1:
                                pexp = se_exp[(p - 1) % 2]
                                for j2 in range(2):
                                    kcp = 2 * (p - 1) + j2
                                    E.emit("tensor", lambda e, kcp=kcp, h=h: e.matmul(
                                        ps_cx[:],
                                        vaug[:, 1040 * kcp + 65 * h: 1040 * kcp + 65 * (h + 1)],
                                        ptS[:, 512 * kcp:512 * (kcp + 1)],
                                        start=(kcp == 0), stop=False, skip_group_check=True),
                                        waits=([(s_se, pexp)] if j2 == 0 else []) +
                                              ([(s_ve, ve_cx[0]), (s_ve, ve_vaug)] if kcp == 0 else []))
                            if p in (1, 2, 3, 4) and pending:
                                pending.pop(0)()
                            if p % 2 == 0 and kmq:
                                kmq.pop(0)()
                        pexp7 = se_exp[7 % 2]
                        mm_pv_done = 0
                        for j2 in range(2):
                            kcp = 14 + j2
                            mm_pv_done = E.emit("tensor", lambda e, kcp=kcp, h=h: e.matmul(
                                ps_cx[:],
                                vaug[:, 1040 * kcp + 65 * h: 1040 * kcp + 65 * (h + 1)],
                                ptS[:, 512 * kcp:512 * (kcp + 1)],
                                start=False, stop=(kcp == 15), skip_group_check=True),
                                waits=([(s_se, pexp7)] if j2 == 0 else []),
                                inc=(s_mm if kcp == 15 else None))
                        if h == 0 and quad == 0 and b == 0 and "pt00" in dbg:
                            E.emit("sync", lambda e: e.dma_start(out=dbg["pt00"], in_=ptS[:]),
                                   waits=[(s_se, se_exp[1])], inc=s_st, incn=16)

                        def mk_ctx_steps(h=h, quad=quad, mm_pv_done=mm_pv_done):
                            def st_evac():
                                ve_cx[0] = E.emit("vector", lambda e: e.tensor_copy(
                                    ctxf[:], ps_cx[0:65, :]),
                                    waits=[(s_mm, mm_pv_done), (s_st, st_ctx_prev)], inc=s_ve)
                            steps = [st_evac]
                            for j in range(4):
                                def st_j(j=j):
                                    tpar = j % 2
                                    tq_idx = 4 * quad + j
                                    mt2 = E.emit("tensor", lambda e, tpar=tpar: e.transpose(
                                        ps_ct[tpar][:], ctxf[:, 128 * j:128 * (j + 1)], identf[0:65, 0:65]),
                                        waits=[(s_ve, ve_cx[0]), (s_ve, ve_ct[tpar])], inc=s_mm)
                                    E.fenced("vector", lambda e, tpar=tpar: e.reciprocal(
                                        rs_t[:, tpar:tpar + 1], ps_ct[tpar][:, 64:65]),
                                        s_ve, waits=[(s_mm, mt2)])
                                    ve_ct[tpar] = E.emit("vector", lambda e, tpar=tpar: e.tensor_scalar(
                                        ctxS[:, DM * tq_idx + 64 * h: DM * tq_idx + 64 * (h + 1)],
                                        ps_ct[tpar][:, 0:64], rs_t[:, tpar:tpar + 1], None, ALU.mult),
                                        inc=s_ve)
                                    ctx_wr[tq_idx] = ve_ct[tpar]
                                steps.append(st_j)
                            return steps
                        pending.extend(mk_ctx_steps())
                for stp in pending:
                    stp()
                pending = []
                st_ctx = [0] * NTC
                for tc in range(NTC):
                    st_ctx[tc] = E.emit("sync", lambda e, tc=tc: e.dma_start(
                        out=ctx_dram[128 * tc:128 * (tc + 1), :], in_=ctxS[:, DM * tc:DM * (tc + 1)]),
                        waits=[(s_ve, ctx_wr[tc]), (s_ldx, st_ctx_rd)], inc=s_st, incn=16)
                st_ctx_prev = s_st._v
                if "ctx" in dbg:
                    E.emit("sync", lambda e: e.dma_start(out=dbg["ctx"], in_=ctxS[:]),
                           waits=[(s_ve, ctx_wr[NTC - 1])], inc=s_st, incn=16)

        with contextlib.ExitStack() as _esd:
            wgT_sb = _esd.enter_context(nc.sbuf_tensor("wgT_sb%d" % b, [128, NKC * IF], BF16))
            wuT_sb = _esd.enter_context(nc.sbuf_tensor("wuT_sb%d" % b, [128, NKC * IF], BF16))
            wdT_sb = _esd.enter_context(nc.sbuf_tensor("wdT_sb%d" % b, [128, NIC * DM], BF16))
            xt2d = _esd.enter_context(nc.sbuf_tensor("xt2d%d" % b, [128, 2 * DM], F32))
            cst1 = _esd.enter_context(nc.sbuf_tensor("cst1%d" % b, [128, DM], BF16))
            x1t2 = _esd.enter_context(nc.sbuf_tensor("x1t2%d" % b, [128, 2 * DM], F32))
            xqf = _esd.enter_context(nc.sbuf_tensor("xqf%d" % b, [128, DM], F32))
            hq_tm = _esd.enter_context(nc.sbuf_tensor("hq_tm%d" % b, [128, IF], BF16))
            xq_tm = _esd.enter_context(nc.sbuf_tensor("xq_tm%d" % b, [128, DM], BF16))
            tT = _esd.enter_context(nc.sbuf_tensor("tT%d" % b, [128, NIC * 128], BF16))
            xqT = tT[:, 0:NKC * 128]
            hqT = tT[:, 0:NIC * 128]
            QTMP["f"] = _esd.enter_context(nc.sbuf_tensor("qdf%d" % b, [128, 3 * 512], F32))
            QTMP["c2"] = _esd.enter_context(nc.sbuf_tensor("qdc2%d" % b, [128, 512], F32))
            QTMP["i"] = _esd.enter_context(nc.sbuf_tensor("qdi%d" % b, [128, 3 * 512], I16))
            QTMP["m"] = _esd.enter_context(nc.sbuf_tensor("qdm%d" % b, [128, 384], F32))
            sgl = tT[:, 1024:2048]
            ps_g = _esd.enter_context(nc.psum_tensor("ps_g%d" % b, [128, 512], F32))
            ps_u = _esd.enter_context(nc.psum_tensor("ps_u%d" % b, [128, 512], F32))
            ps_h = (_esd.enter_context(nc.psum_tensor("ps_h0%d" % b, [128, 512], F32)),
                    _esd.enter_context(nc.psum_tensor("ps_h1%d" % b, [128, 512], F32)))
            ps_dn0 = _esd.enter_context(nc.psum_tensor("ps_dn0%d" % b, [128, 512], F32))
            ps_dn1 = _esd.enter_context(nc.psum_tensor("ps_dn1%d" % b, [128, 512], F32))
            ps_f0 = _esd.enter_context(nc.psum_tensor("ps_f0%d" % b, [128, 128], BF16))
            ps_f1 = _esd.enter_context(nc.psum_tensor("ps_f1%d" % b, [128, 128], BF16))
            ps_f = (ps_f0, ps_f1)
            for kc in range(NKC):
                E.emit("sync", lambda e, kc=kc: e.dma_start(
                    out=wgT_sb[:, IF * kc:IF * (kc + 1)].rearrange("p (r s) -> p r s", r=8),
                    in_=wQ_ld["g"][kc]),
                    waits=[(s_cc, cc_wq_done)], inc=s_ld, incn=16)
            for kc in range(NKC):
                E.emit("sync", lambda e, kc=kc: e.dma_start(
                    out=wuT_sb[:, IF * kc:IF * (kc + 1)].rearrange("p (r s) -> p r s", r=8),
                    in_=wQ_ld["u"][kc]),
                    inc=s_ld, incn=16)
            lwu = s_ld._v
            for kc in range(NIC):
                E.emit("sync", lambda e, kc=kc: e.dma_start(
                    out=wdT_sb[:, DM * kc:DM * (kc + 1)], in_=wQ_ld["d"][128 * kc:128 * (kc + 1), :]),
                    inc=s_ld, incn=16)
            lwd = s_ld._v
            ve_x1_free = 0
            for tc in range(NTC):
                par = tc % 2
                lx = E.emit("sync", lambda e, b=b, tc=tc, par=par: e.dma_start(
                    out=xt2d[:, DM * par:DM * (par + 1)], in_=x_d[b, 128 * tc:128 * (tc + 1), :]),
                    waits=[(s_ve, ve_x1_free)], inc=s_ldx, incn=16)
                lc = E.emit("sync", lambda e, tc=tc: e.dma_start(
                    out=cst1[:], in_=ctx_dram[128 * tc:128 * (tc + 1), :]),
                    waits=[(s_st, st_ctx[tc]), (s_ve, ve_x1_free)], inc=s_ldx, incn=16)
                x1v = E.emit("vector", lambda e, par=par: e.tensor_tensor(
                    x1t2[:, 0:DM], xt2d[:, DM * par:DM * (par + 1)], cst1[:], ALU.add),
                    waits=[(s_ldx, lc), (s_se, None)], inc=s_ve)
                ve_x1_free = emit_ln(x1t2[:, 0:DM], mrow[:, tc:tc + 1], rrow[:, tc:tc + 1],
                                     nbias[:, tc:tc + 1], waits0=[(s_ve, x1v)])

            se_evT = [0, 0]
            st_out_par = [0, 0]
            xq_ready = [0] * NTC
            ve_xq_t = [0] * NTC
            x1v_t = [0] * NTC
            sgv_last = [0]
            hv_last = [0]
            hq_mt_last = [0]
            ve_out_last = [0]
            lc_prev = [s_ve._v]
            ve_pass1 = s_ve._v

            def tile_front(tc):
                par = tc % 2
                lx = E.emit("sync", lambda e, b=b, tc=tc, par=par: e.dma_start(
                    out=xt2d[:, DM * par:DM * (par + 1)], in_=x_d[b, 128 * tc:128 * (tc + 1), :]),
                    waits=[(s_st, st_out_par[par]), (s_ve, x1v_t[tc - 2] if tc >= 2 else ve_pass1)],
                    inc=s_ldx, incn=16)
                lc = E.emit("sync", lambda e, tc=tc: e.dma_start(
                    out=cst1[:], in_=ctx_dram[128 * tc:128 * (tc + 1), :]),
                    waits=[(s_st, st_ctx[tc]), (s_ve, lc_prev[0])], inc=s_ldx, incn=16)
                x1 = x1t2[:, DM * par:DM * (par + 1)]
                x1v = E.emit("vector", lambda e, x1=x1, par=par: e.tensor_tensor(
                    x1, xt2d[:, DM * par:DM * (par + 1)], cst1[:], ALU.add),
                    waits=[(s_ldx, lc), (s_se, None)], inc=s_ve)
                lc_prev[0] = x1v
                x1v_t[tc] = x1v
                xl2 = E.emit("scalar", lambda e, x1=x1, tc=tc: e.activation(
                    xqf[:], x1, AF.Identity, bias=nbias[:, tc:tc + 1], scale=rrow[:, tc:tc + 1]),
                    waits=[(s_ve, x1v)], inc=s_se)
                ve_xq_t[tc] = quant_pair(
                    (xqf[:, 0:512], xq_tm[:, 0:512], 512, QTMP, 0,
                     [(s_se, xl2)], [(s_mm, xq_mt_last[0])]),
                    (xqf[:, 512:1024], xq_tm[:, 512:1024], 512, QTMP, 1,
                     [(s_se, xl2)], [(s_mm, xq_mt_last[0])]))

            xq_mt_last = [0]

            def tile_xqT(tc):
                for c in range(NKC):
                    tpar = c % 2
                    mt3 = E.emit("tensor", lambda e, c=c, tpar=tpar: e.transpose(
                        ps_f[tpar][:], xq_tm[:, 128 * c:128 * (c + 1)], identb[:]),
                        waits=[(s_ve, ve_xq_t[tc]), (s_se, se_evT[tpar])], inc=s_mm)
                    se_evT[tpar] = E.emit("scalar", lambda e, c=c, tpar=tpar: e.activation(
                        xqT[:, 128 * c:128 * (c + 1)], ps_f[tpar][:], AF.Copy),
                        waits=[(s_mm, mt3)], inc=s_se)
                    xq_mt_last[0] = mt3
                xq_ready[tc] = s_se._v

            def tile_matmuls(tc):
                par = tc % 2
                x1 = x1t2[:, DM * par:DM * (par + 1)]
                hq_used = [0, 0]
                for gp in range(0, len(GRPS), 2):
                    pr = []
                    for gi in range(gp, min(gp + 2, len(GRPS))):
                        g0, gw = GRPS[gi]
                        hp = gi % 2
                        def mmg(e, g0=g0, gw=gw):
                            ins = None
                            for kc in range(NKC):
                                ins = e.matmul(ps_g[:, 0:gw], xqT[:, 128 * kc:128 * (kc + 1)],
                                               wgT_sb[:, IF * kc + g0: IF * kc + g0 + gw],
                                               start=(kc == 0), stop=(kc == NKC - 1), skip_group_check=True)
                            return ins
                        def mmu(e, g0=g0, gw=gw):
                            ins = None
                            for kc in range(NKC):
                                ins = e.matmul(ps_u[:, 0:gw], xqT[:, 128 * kc:128 * (kc + 1)],
                                               wuT_sb[:, IF * kc + g0: IF * kc + g0 + gw],
                                               start=(kc == 0), stop=(kc == NKC - 1), skip_group_check=True)
                            return ins
                        mg = E.emit("tensor", mmg,
                                    waits=[(s_se, max(xq_ready[tc], sgv_last[0])), (s_ld, lwu)], inc=s_mm)
                        mu = E.emit("tensor", mmu, waits=[(s_ve, hv_last[0])], inc=s_mm)
                        sgv = E.emit("scalar", lambda e, gw=gw, hp=hp: e.activation(
                            sgl[:, 512 * hp:512 * hp + gw], ps_g[:, 0:gw], AF.Silu),
                            waits=[(s_mm, mg)], inc=s_se)
                        sgv_last[0] = sgv
                        hv = E.emit("vector", lambda e, gw=gw, hp=hp: e.tensor_tensor(
                            ps_h[hp][:, 0:gw], sgl[:, 512 * hp:512 * hp + gw], ps_u[:, 0:gw], ALU.mult),
                            waits=[(s_se, sgv), (s_mm, mu), (s_ve, hq_used[hp])], inc=s_ve)
                        hv_last[0] = hv
                        pr.append((ps_h[hp][:, 0:gw], hq_tm[:, g0:g0 + gw], gw, QTMP, hp,
                                   [(s_ve, hv)], [(s_mm, hq_mt_last[0])]))
                    vq = quant_pair(pr[0], pr[1] if len(pr) > 1 else None)
                    hq_used[0] = hq_used[1] = vq
                for c in range(NIC):
                    tpar = c % 2
                    mt4 = E.emit("tensor", lambda e, c=c, tpar=tpar: e.transpose(
                        ps_f[tpar][:], hq_tm[:, 128 * c:128 * (c + 1)], identb[:]),
                        waits=[(s_ve, None), (s_se, se_evT[tpar])], inc=s_mm)
                    se_evT[tpar] = E.emit("scalar", lambda e, c=c, tpar=tpar: e.activation(
                        hqT[:, 128 * c:128 * (c + 1)], ps_f[tpar][:], AF.Copy),
                        waits=[(s_mm, mt4)], inc=s_se)
                    hq_mt_last[0] = mt4
                se_hqT = s_se._v
                def mmd(e):
                    ins = None
                    for half in range(2):
                        for kc in range(NIC):
                            ins = e.matmul((ps_dn0 if half == 0 else ps_dn1)[:],
                                           hqT[:, 128 * kc:128 * (kc + 1)],
                                           wdT_sb[:, DM * kc + 512 * half: DM * kc + 512 * (half + 1)],
                                           start=(kc == 0), stop=(kc == NIC - 1), skip_group_check=True)
                    return ins
                md = E.emit("tensor", mmd,
                            waits=[(s_se, se_hqT), (s_ld, lwd), (s_ve, ve_out_last[0])], inc=s_mm)
                outb = xt2d[:, DM * par:DM * (par + 1)]
                E.emit("vector", lambda e, x1=x1, outb=outb: e.tensor_tensor(
                    outb[:, 0:512], x1[:, 0:512], ps_dn0[:], ALU.add),
                    waits=[(s_mm, md)], inc=s_ve)
                ov = E.emit("vector", lambda e, x1=x1, outb=outb: e.tensor_tensor(
                    outb[:, 512:1024], x1[:, 512:1024], ps_dn1[:], ALU.add),
                    inc=s_ve)
                ve_out_last[0] = ov
                st_out_par[par] = E.emit("sync", lambda e, b=b, tc=tc, par=par, outb=outb: e.dma_start(
                    out=out_d[b, 128 * tc:128 * (tc + 1), :], in_=outb),
                    waits=[(s_ve, ov)], inc=s_st, incn=16)

            for tc in range(NTC):
                tile_front(tc)
                if tc >= 1:
                    tile_matmuls(tc - 1)
                tile_xqT(tc)
            tile_matmuls(NTC - 1)
            st_ctx_rd = s_ldx._v
        ve_xt_free_start = s_ve._v

    E.wait_only("sync", [(s_st, s_st._v)])

    with nc.Block() as block:
        E.run(block)
    return nc, dbg



from concourse.bass_utils import run_bass_kernel_spmd

_N_CORES = 8
_NB = 16 // _N_CORES


def kernel(L0_attn_hidden_states, L0_attn_Wq, L0_attn_bq, L0_attn_Wk, L0_attn_bk,
           L0_attn_Wv, L0_attn_bv, L0_attn_coefficients, L0_attn_q_proj_weight,
           L0_attn_k_proj_weight, L0_ffn_w_gate, L0_ffn_w_up, L0_ffn_w_down):
    x = np.ascontiguousarray(np.asarray(L0_attn_hidden_states, dtype=np.float32))
    shared = {
        "wq": np.ascontiguousarray(np.asarray(L0_attn_Wq, dtype=np.float32)),
        "bq": np.ascontiguousarray(np.asarray(L0_attn_bq, dtype=np.float32)),
        "wk": np.ascontiguousarray(np.asarray(L0_attn_Wk, dtype=np.float32)),
        "bk": np.ascontiguousarray(np.asarray(L0_attn_bk, dtype=np.float32)),
        "wv": np.ascontiguousarray(np.asarray(L0_attn_Wv, dtype=np.float32)),
        "bv": np.ascontiguousarray(np.asarray(L0_attn_bv, dtype=np.float32)),
        "coef": np.ascontiguousarray(np.asarray(L0_attn_coefficients, dtype=np.float32)),
        "qproj": np.ascontiguousarray(np.asarray(L0_attn_q_proj_weight, dtype=np.float32)),
        "kproj": np.ascontiguousarray(np.asarray(L0_attn_k_proj_weight, dtype=np.float32)),
    }
    shared.update(host_constants())
    wg = np.asarray(L0_ffn_w_gate, dtype=np.float32)
    wu = np.asarray(L0_ffn_w_up, dtype=np.float32)
    wd = np.asarray(L0_ffn_w_down, dtype=np.float32)
    nc, _ = build_core_program(NB=_NB)
    in_maps = []
    for c in range(_N_CORES):
        m = dict(shared)
        m["x"] = np.ascontiguousarray(x[_NB * c:_NB * (c + 1)])
        m["wgsh"] = np.ascontiguousarray(wg[352 * c:352 * (c + 1)])
        m["wush"] = np.ascontiguousarray(wu[352 * c:352 * (c + 1)])
        m["wdsh"] = np.ascontiguousarray(wd[:, 352 * c:352 * (c + 1)])
        in_maps.append(m)
    res = run_bass_kernel_spmd(nc, in_maps, core_ids=list(range(_N_CORES)))
    out = np.concatenate([np.asarray(res.results[c]["out"]) for c in range(_N_CORES)], axis=0)
    return out.astype(np.float32, copy=False)
